# version 33
# baseline (speedup 1.0000x reference)
import numpy as np
import ml_dtypes
from contextlib import ExitStack
import concourse.bass as bass
import concourse.mybir as mybir
from concourse.bass_utils import run_bass_kernel_spmd

F32 = mybir.dt.float32
BF16 = mybir.dt.bfloat16
I32 = mybir.dt.int32
AF = mybir.ActivationFunctionType
ALU = mybir.AluOpType

B, T, C, H, D, FF = 32, 2048, 1024, 16, 64, 3072
NCORES = 8
BSEQ = B // NCORES
SCW = 512
NSC = T // SCW
LK = 128
NKC = SCW // LK
WF = 256
NWIN = FF // WF
NFT = FF // 128
LN_EPS = 1e-5
GN_EPS = 1e-5
INV127 = 1.0 / 127.0

ENGS = ("sync", "gpsimd", "scalar", "vector", "tensor")


def _gammas():
    lo = float(np.log(1.0 - 0.999))
    hi = float(np.log(1.0 - 0.9))
    lin = np.linspace(lo, hi, H, dtype=np.float32)
    return (1.0 - np.exp(lin.astype(np.float64)))


def make_consts():
    g = _gammas()
    i = np.arange(SCW, dtype=np.float64)
    qdec = np.zeros((128, 8 * SCW), dtype=np.float32)
    kdec = np.zeros((128, 8 * SCW), dtype=np.float32)
    for pc in range(8):
        for half in range(2):
            h = 2 * pc + half
            rows = slice(64 * half, 64 * half + 64)
            qdec[rows, SCW * pc:SCW * (pc + 1)] = (g[h] ** i).astype(np.float32)
            kdec[rows, SCW * pc:SCW * (pc + 1)] = (g[h] ** (-i)).astype(np.float32)
    mask = np.ones((128, SCW), dtype=np.float32)
    mask[:, :LK] = np.tril(np.ones((LK, LK), dtype=np.float32)).T
    sdec = np.zeros((128, 512), dtype=np.float32)
    for pair in range(8):
        for half in range(2):
            h = 2 * pair + half
            sdec[64 * half:64 * half + 64, 64 * pair:64 * pair + 64] = \
                np.float32(g[h] ** SCW)
    sel = np.zeros((128, 2), dtype=np.float32)
    sel[:64, 0] = 1.0
    sel[64:, 1] = 1.0
    selT = np.ascontiguousarray(sel.T)
    ident = np.eye(128, dtype=np.float32).astype(ml_dtypes.bfloat16)
    epsb = np.full((128, 1), LN_EPS, dtype=np.float32)
    return dict(qdec=qdec, kdec=kdec, mask=mask, sdec=sdec, sel=sel,
                selT=selT, ident=ident, epsb=epsb)


class Emit:

    def __init__(self):
        self.q = {e: [] for e in ENGS}
        self.qmark = {e: [] for e in ENGS}
        self.curmark = "start"
        self.cnt = {}
        self.handles = {}
        self.producer = {}
        self.seen = {e: {} for e in ENGS}

    def sem(self, name, handle, producer):
        self.handles[name] = handle
        self.cnt[name] = 0
        self.producer[name] = producer

    def pool(self, pname, lane_names):
        self.pools = getattr(self, "pools", {})
        self.pools[pname] = {"lanes": lane_names, "rr": 0,
                             "uses": [0] * len(lane_names)}

    def dma(self, eng, pname, fn, waits=()):
        st = self.pools[pname]
        i = st["rr"] % len(st["lanes"])
        st["rr"] += 1
        lane = st["lanes"][i]
        st["uses"][i] += 1
        u = st["uses"][i]
        w = list(waits) + [(lane, 16 * (u - 1))]
        return self.op(eng, fn, waits=w, inc=lane, inc_n=16)

    def tok(self, name):
        return (name, self.cnt[name])

    def _flatten(self, waits, out):
        for wv in waits:
            if wv is None:
                continue
            if isinstance(wv, list):
                self._flatten(wv, out)
            else:
                out.append(wv)
        return out

    def _waitlist(self, eng, waits):
        wl = []
        for wv in self._flatten(list(waits), []):
            s, v = wv
            if v <= 0 or (self.producer[s] == eng and eng == "tensor"):
                continue
            if self.seen[eng].get(s, -1) >= v:
                continue
            self.seen[eng][s] = v
            wl.append((self.handles[s], v))
        return wl

    def op(self, eng, fn, waits=(), inc=None, inc_n=1):
        wl = self._waitlist(eng, waits)
        tok = None
        h = n = None
        if inc is not None:
            self.cnt[inc] += inc_n
            tok = (inc, self.cnt[inc])
            h, n = self.handles[inc], inc_n

        def runner(e, wl=wl, fn=fn, h=h, n=n):
            for (sh, sv) in wl:
                e.wait_ge(sh, sv)
            inst = fn(e)
            if h is not None:
                inst.then_inc(h, n)
        self.q[eng].append(runner)
        self.qmark[eng].append(self.curmark)
        return tok

    def wait(self, eng, waits):
        wl = self._waitlist(eng, waits)
        if wl:
            def runner(e, wl=wl):
                for (sh, sv) in wl:
                    e.wait_ge(sh, sv)
            self.q[eng].append(runner)


def build_nc(n_seq=BSEQ, n_sc=NSC):
    nc = bass.Bass("TRN2", target_bir_lowering=False, debug=False)
    dram = {}
    dram["x"] = nc.dram_tensor("x", [BSEQ, T, C], F32, kind="ExternalInput").ap()
    for nm in ("wqT", "wkT", "wvT", "woT"):
        dram[nm] = nc.dram_tensor(nm, [C, C], F32, kind="ExternalInput").ap()
    for nm in ("wgT", "wuT"):
        dram[nm] = nc.dram_tensor(nm, [C, FF], I32, kind="ExternalInput").ap()
    dram["wdT"] = nc.dram_tensor("wdT", [FF, C], I32, kind="ExternalInput").ap()
    for nm, shp in (("gnw", [128, 8]), ("gnb", [128, 8]),
                    ("qdec", [128, 8 * SCW]), ("kdec", [128, 8 * SCW]),
                    ("mask", [128, SCW]), ("sdec", [128, 512]),
                    ("sel", [128, 2]), ("selT", [2, 128]),
                    ("epsb", [128, 1])):
        dram[nm] = nc.dram_tensor(nm, shp, F32, kind="ExternalInput").ap()
    dram["ident"] = nc.dram_tensor("ident", [128, 128], BF16,
                                   kind="ExternalInput").ap()
    dram["out"] = nc.dram_tensor("out", [BSEQ, T, C], F32,
                                 kind="ExternalOutput").ap()
    dbg = "ExternalOutput" if globals().get("DEBUG_SCRATCH") else "Internal"
    for nm, shp in (("s_wq", [C, C]), ("s_wk", [C, C]), ("s_wv", [C, C]),
                    ("s_wo", [C, C]), ("s_wg", [C, FF]), ("s_wu", [C, FF]),
                    ("s_wd", [FF, C])):
        dram[nm] = nc.dram_tensor(nm, shp, BF16, kind=dbg).ap()
    dram["s_y"] = nc.dram_tensor("s_y", [BSEQ, C, T], BF16, kind=dbg).ap()
    dram["s_acc"] = nc.dram_tensor("s_acc", [BSEQ, T, C], F32, kind=dbg).ap()
    if globals().get("DEBUG_SCRATCH"):
        dram["d_ha"] = nc.dram_tensor("d_ha", [NSC, 128, 12288], BF16,
                                      kind="ExternalOutput").ap()

    with ExitStack() as ctx:
        sb = {}

        def S(name, shape, dt):
            sb[name] = ctx.enter_context(nc.sbuf_tensor("sb_" + name, shape, dt))

        S("xb0", [128, 4096], F32)
        S("xb1", [128, 4096], F32)
        S("junk", [128, 1024], F32)
        S("lnst", [128, 16], F32)
        S("gn_acc", [128, 64], F32)
        S("gn_rows", [2, 64], F32)
        S("gn_der", [2, 48], F32)
        S("ab", [128, 16], F32)
        S("Shat", [128, 512], F32)
        for nm, shp in (("qdec", [128, 4096]), ("kdec", [128, 4096]),
                        ("mask", [128, SCW]), ("sdec", [128, 512]),
                        ("sel", [128, 2]), ("selT", [2, 128]),
                        ("gnw", [128, 8]), ("gnb", [128, 8]),
                        ("epsb", [128, 1])):
            S(nm, shp, F32)
        S("xn", [128, 4096], BF16)
        S("qt", [128, 4096], BF16)
        S("kt", [128, 4096], BF16)
        S("vb", [128, 4096], BF16)
        S("ha", [128, 12288], BF16)
        S("yb", [128, 4096], BF16)
        S("silu2", [128, 1024], BF16)
        S("Sbf", [128, 512], BF16)
        S("wset0", [128, 8192], BF16)
        S("wset1", [128, 8192], BF16)
        S("wgw", [128, 4096], BF16)
        S("wuw", [128, 4096], BF16)
        S("wdr", [128, 8192], BF16)
        S("ident", [128, 128], BF16)
        pb = [ctx.enter_context(nc.psum_tensor(f"pb{i}", [128, 512], F32))
              for i in range(8)]
        pbt = None

        E = Emit()
        for snm, prod in (("P", "tensor"), ("V", "vector"), ("A", "scalar"),
                          ("O", "gpsimd")):
            E.sem(snm, ctx.enter_context(nc.semaphore(snm)), prod)
        for pname, nlanes in (("W", 8), ("X", 4), ("G", 6)):
            lanes = []
            for i in range(nlanes):
                ln = f"{pname}{i}"
                E.sem(ln, ctx.enter_context(nc.semaphore(ln)), None)
                lanes.append(ln)
            E.pool(pname, lanes)

        emit_program(E, dram, sb, pb, pbt, n_seq, n_sc)
        build_nc.last_marks = (list(E.phasemarks), E.mmcount)
        build_nc.last_E = E

        with nc.Block() as block:
            @block.sync
            def _(e):
                for f in E.q["sync"]:
                    f(e)

            @block.gpsimd
            def _(e):
                for f in E.q["gpsimd"]:
                    f(e)

            @block.scalar
            def _(e):
                for f in E.q["scalar"]:
                    f(e)

            @block.vector
            def _(e):
                for f in E.q["vector"]:
                    f(e)

            @block.tensor
            def _(e):
                for f in E.q["tensor"]:
                    f(e)
    return nc


def emit_program(E, dram, sb, pb, pbt, n_seq, n_sc):
    ctok = {}
    for nm in ("ident", "epsb", "sel", "selT", "gnw", "gnb", "mask", "sdec",
               "qdec", "kdec"):
        ctok[nm] = E.dma("sync", "W", lambda e, nm=nm: e.dma_start(
            out=sb[nm][:], in_=dram[nm]))

    stage_done = {}

    xb = [sb["xb0"], sb["xb1"]]
    wset = [sb["wset0"], sb["wset1"]]
    x_tok = {}
    xbuf_free = [None, None]
    wset_where = {}
    bankfree = {i: None for i in range(8)}
    pbtfree = [None, None]
    win_tok = {}
    win_free = None
    wd_free = [None] * 8
    wd_pair = [0]
    yscr_tok = {}
    accscr_tok = {}
    out_toks = []
    sbf_tok = None
    xn_free = None
    ha_free = None
    ysb_free = None
    lnst_free = None
    junk_sq_free = [None, None]
    gn_prev = None

    E.mmcount = 0
    E.phasemarks = []

    def mark(name):
        E.phasemarks.append((E.mmcount, name))
        E.curmark = name

    def mm(waits, out, lhsT, rhs, start, stop, inc=None):
        E.mmcount += 1
        return E.op("tensor",
                    lambda e, o=out, l=lhsT, r=rhs, s1=start, s2=stop:
                    e.matmul(o, l, r, start=s1, stop=s2),
                    waits=waits, inc=inc)

    def tp(waits, out, in_, inc=None):
        E.mmcount += 1
        return E.op("tensor", lambda e, o=out, i=in_:
                    e.matmul(o, i, sb["ident"][:], start=True, stop=True),
                    waits=waits, inc=inc)

    def load_x(seq, sc):
        par = (seq * n_sc + sc) % 2
        free = xbuf_free[par]
        toks = []
        for mt in range(4):
            fr = free[mt] if (isinstance(free, list) and len(free) == 4) \
                else free
            toks.append(E.dma("scalar", "X", lambda e,
                              o=xb[par][:, 1024 * mt:1024 * (mt + 1)],
                              i=dram["x"][seq, SCW * sc + 128 * mt:
                                          SCW * sc + 128 * (mt + 1), :]:
                              e.dma_start(out=o, in_=i), waits=[fr]))
        x_tok[(seq, sc)] = toks

    def load_wset(slot, snm, free_tok):
        src = dram[snm][:, :].rearrange("(kc p) n -> p kc n", p=128)
        dst = wset[slot][:].rearrange("p (kc n) -> p kc n", kc=8)
        t = E.dma("sync", "W", lambda e, o=dst, i=src:
                  e.dma_start(out=o, in_=i),
                  waits=[stage_done[snm], free_tok])
        wset_where[snm] = (slot, t)
        return t

    def load_win(w, free_tok):
        toks = []
        for snm, buf in (("s_wg", "wgw"), ("s_wu", "wuw")):
            src = dram[snm][:, WF * w:WF * (w + 1)].rearrange(
                "(kc p) f -> p kc f", p=128)
            dst = sb[buf][:, 2048 * (w % 2):2048 * (w % 2 + 1)].rearrange(
                "p (kc f) -> p kc f", kc=8)
            toks.append(E.dma("sync", "W", lambda e, o=dst, i=src:
                              e.dma_start(out=o, in_=i),
                              waits=[stage_done[(snm, 0 if w < 4 else 1)],
                                     free_tok]))
        win_tok[w] = toks



    seq_ab_tok = None
    st3 = None
    rot = 0

    pend_xn = {}

    def emit_ln_chain(seq_i, sc_i, extra=()):
        st = {}
        xc = xb[(seq_i * n_sc + sc_i) % 2]
        xt_i = x_tok[(seq_i, sc_i)]

        def p_sum(a, b2):
            def f():
                for mt in (a, b2):
                    st["vsum"] = E.op("vector", lambda e, t=xc[
                        :, 1024 * mt:1024 * (mt + 1)], c=mt: e.reduce_sum(
                        sb["lnst"][:, c:c + 1], t, axis=mybir.AxisListType.X),
                        waits=[xt_i[mt], lnst_free], inc="V")
            return f

        def p_sq(mt):
            def f():
                a_sq = E.op("scalar", lambda e, t=xc[
                    :, 1024 * mt:1024 * (mt + 1)]: e.activation(
                    sb["junk"][:], t, AF.Square),
                    waits=[xt_i[mt], st.get("vsq"), junk_sq_free[0],
                           junk_sq_free[1]], inc="A")
                st["vsq"] = E.op("vector", lambda e, c=mt: e.reduce_sum(
                    sb["lnst"][:, 4 + c:5 + c], sb["junk"][:],
                    axis=mybir.AxisListType.X), waits=[a_sq], inc="V")
            return f

        def p_small():
            def f():
                nonlocal_state = None
                t1 = E.op("vector", lambda e: e.tensor_scalar(
                    sb["lnst"][:, 0:4], sb["lnst"][:, 0:4], 1.0 / C, 0.0,
                    ALU.mult, ALU.add), waits=[st["vsum"], st["vsq"]], inc="V")
                t2 = E.op("vector", lambda e: e.tensor_scalar(
                    sb["lnst"][:, 4:8], sb["lnst"][:, 4:8], 1.0 / C, 0.0,
                    ALU.mult, ALU.add), waits=[t1], inc="V")
                t3 = E.op("vector", lambda e: e.tensor_mul(
                    sb["lnst"][:, 8:12], sb["lnst"][:, 0:4],
                    sb["lnst"][:, 0:4]), waits=[t2], inc="V")
                v_var = E.op("vector", lambda e: e.tensor_sub(
                    sb["lnst"][:, 4:8], sb["lnst"][:, 4:8],
                    sb["lnst"][:, 8:12]), waits=[t3], inc="V")
                a_sqrt = E.op("scalar", lambda e: e.activation(
                    sb["lnst"][:, 8:12], sb["lnst"][:, 4:8], AF.Sqrt,
                    bias=sb["epsb"][:, 0:1]),
                    waits=[v_var, ctok["epsb"]], inc="A")
                t4 = E.op("vector", lambda e: e.reciprocal(
                    sb["lnst"][:, 8:12], sb["lnst"][:, 8:12]),
                    waits=[a_sqrt], inc="V")
                st["v_ln"] = E.op("vector", lambda e: e.scalar_tensor_tensor(
                    sb["lnst"][:, 12:16], sb["lnst"][:, 0:4], -1.0,
                    sb["lnst"][:, 8:12], ALU.mult, ALU.mult),
                    waits=[t4], inc="V")
            return f

        def p_aff(mt):
            def f():
                tmsl = sb["vb"][:, 1024 * mt:1024 * (mt + 1)]
                st.setdefault("aff", []).append(E.op(
                    "gpsimd", lambda e,
                    t=xc[:, 1024 * mt:1024 * (mt + 1)],
                    o=tmsl, c=mt: e.tensor_scalar(
                        o, t,
                        sb["lnst"][:, 8 + c:9 + c],
                        sb["lnst"][:, 12 + c:13 + c],
                        ALU.mult, ALU.add),
                    waits=[st["v_ln"], tp_done[0], tp_done[1], xt_i[mt]]
                    + list(extra), inc="O"))
            return f

        def finish():
            lnst_done(st)
            return st["aff"]
        pieces = [p_sum(0, 1), p_sum(2, 3), p_sq(0), p_sq(1), p_sq(2),
                  p_sq(3), p_small(), p_aff(0), p_aff(1), p_aff(2), p_aff(3)]
        return pieces, st

    def lnst_done(st):
        nonlocal lnst_free, junk_sq_free
        lnst_free = st["v_ln"]
        junk_sq_free = [st["vsq"], st["vsq"]]

    def run_ln(seq_i, sc_i, extra=()):
        pieces, st = emit_ln_chain(seq_i, sc_i, extra)
        for p in pieces:
            p()
        lnst_done(st)
        return st["aff"]

    tp_done = [None, None]

    def emit_xn_tp(aff_toks):
        nonlocal rot
        xn_g0 = xn_ready = None
        last_tp = None
        for g in range(2):
            for mt in range(4):
                tmsl = sb["vb"][:, 1024 * mt:1024 * (mt + 1)]
                bk = rot % 8
                rot += 1
                for j in range(4):
                    p = 4 * g + j
                    last_tp = tp([aff_toks[mt], ctok["ident"],
                                  bankfree[bk], xn_free] if j == 0 else [],
                                 pb[bk][:, 128 * j:128 * (j + 1)],
                                 tmsl[:, 128 * p:128 * (p + 1)],
                                 inc="P" if j == 3 else None)
                xo = sb["xn"][:].rearrange("p (k t) -> p k t", k=8)[
                    :, 4 * g:4 * g + 4, 128 * mt:128 * (mt + 1)]
                xi_ap = pb[bk][:].rearrange("p (j t) -> p j t", j=4)
                xn_ready = E.op("scalar", lambda e, o=xo, i=xi_ap:
                                e.activation(o, i, AF.Copy),
                                waits=[last_tp], inc="A")
                bankfree[bk] = xn_ready
            if g == 0:
                xn_g0 = xn_ready
        tp_done[0] = last_tp
        tp_done[1] = last_tp
        return xn_g0, xn_ready

    def emit_gn(gn_last_tok):
        nonlocal seq_ab_tok
        denom = 1.0 / float(n_sc * SCW * D)
        gst = mm([gn_last_tok, ctok["sel"], bankfree[3]],
                 pb[3][0:2, 0:64], sb["sel"][:], sb["gn_acc"][:],
                 True, True, inc="P")
        g0 = E.op("vector", lambda e: e.tensor_copy(
            sb["gn_rows"][:], pb[3][0:2, 0:64]), waits=[gst], inc="V")
        bankfree[3] = g0
        g1 = E.op("vector", lambda e: e.tensor_add(
            sb["gn_der"][:, 0:16], sb["gn_rows"][:, 0:16],
            sb["gn_rows"][:, 16:32]), waits=[g0], inc="V")
        g2 = E.op("vector", lambda e: e.tensor_add(
            sb["gn_der"][:, 0:16], sb["gn_der"][:, 0:16],
            sb["gn_rows"][:, 32:48]), waits=[g1], inc="V")
        g3 = E.op("vector", lambda e: e.tensor_add(
            sb["gn_der"][:, 0:16], sb["gn_der"][:, 0:16],
            sb["gn_rows"][:, 48:64]), waits=[g2], inc="V")
        gv = sb["gn_der"][:].rearrange("p (a b) -> p a b", b=2)
        g4 = E.op("vector", lambda e: e.tensor_scalar(
            sb["gn_der"][:, 16:24].rearrange("p (a b) -> p a b", b=1),
            gv[:, 0:8, 0:1], denom, 0.0, ALU.mult, ALU.add),
            waits=[g3], inc="V")
        g5 = E.op("vector", lambda e: e.tensor_scalar(
            sb["gn_der"][:, 24:32].rearrange("p (a b) -> p a b", b=1),
            gv[:, 0:8, 1:2], denom, 0.0, ALU.mult, ALU.add),
            waits=[g4], inc="V")
        g6 = E.op("vector", lambda e: e.tensor_mul(
            sb["gn_der"][:, 32:40], sb["gn_der"][:, 16:24],
            sb["gn_der"][:, 16:24]), waits=[g5], inc="V")
        vv = E.op("vector", lambda e: e.tensor_sub(
            sb["gn_der"][:, 24:32], sb["gn_der"][:, 24:32],
            sb["gn_der"][:, 32:40]), waits=[g6], inc="V")
        aa = E.op("scalar", lambda e: e.activation(
            sb["gn_der"][:, 32:40], sb["gn_der"][:, 24:32], AF.Sqrt,
            bias=sb["epsb"][0:2, 0:1]), waits=[vv], inc="A")
        g7 = E.op("vector", lambda e: e.reciprocal(
            sb["gn_der"][:, 32:40], sb["gn_der"][:, 32:40]), waits=[aa],
            inc="V")
        gder = E.op("vector", lambda e: e.tensor_mul(
            sb["gn_der"][:, 40:48], sb["gn_der"][:, 16:24],
            sb["gn_der"][:, 32:40]), waits=[g7], inc="V")
        mm([gder, ctok["selT"], bankfree[4]],
           pb[4][:, 0:8], sb["selT"][:], sb["gn_der"][:, 32:40], True, True)
        gbc = mm([], pb[4][:, 8:16], sb["selT"][:], sb["gn_der"][:, 40:48],
                 True, True, inc="P")
        b0 = E.op("vector", lambda e: e.tensor_mul(
            sb["ab"][:, 0:8], pb[4][:, 0:8], sb["gnw"][:]),
            waits=[gbc, ctok["gnw"], ctok["gnb"]], inc="V")
        b1 = E.op("vector", lambda e: e.scalar_tensor_tensor(
            sb["ab"][:, 8:16], pb[4][:, 8:16], -1.0, sb["gnw"][:],
            ALU.mult, ALU.mult), waits=[b0], inc="V")
        seq_ab_tok = E.op("vector", lambda e: e.tensor_add(
            sb["ab"][:, 8:16], sb["ab"][:, 8:16], sb["gnb"][:]),
            waits=[b1], inc="V")
        bankfree[4] = seq_ab_tok


    def emit_pb_prep(seq_i, scb, xnbuf, ybbuf, xnfree_tok, ybfree):
        parb = (seq_i * n_sc + scb) % 2
        accb = xb[parb]
        ysrc2 = dram["s_y"][seq_i, :, SCW * scb:SCW * (scb + 1)].rearrange(
            "(pc p) t -> p pc t", p=128)
        ydst2 = ybbuf[:].rearrange("p (pc t) -> p pc t", pc=8)
        ylt = E.dma("scalar", "X", lambda e, o=ydst2, i=ysrc2:
                    e.dma_start(out=o, in_=i),
                    waits=[yscr_tok[(seq_i, scb)]] + ybfree)
        asrc2 = dram["s_acc"][seq_i, SCW * scb:SCW * (scb + 1), :].rearrange(
            "(mt p) c -> p mt c", p=128)
        adst2 = accb[:].rearrange("p (mt c) -> p mt c", mt=4)
        alt = E.dma("scalar", "X", lambda e, o=adst2, i=asrc2:
                    e.dma_start(out=o, in_=i),
                    waits=[accscr_tok[(seq_i, scb)], xbuf_free[parb]])
        yp_tok = None
        for pc in range(8):
            yp_tok = E.op("gpsimd", lambda e, pc=pc, xnb=xnbuf, ybb=ybbuf:
                          e.tensor_scalar(
                              xnb[:, 512 * pc:512 * (pc + 1)],
                              ybb[:, 512 * pc:512 * (pc + 1)],
                              sb["ab"][:, pc:pc + 1],
                              sb["ab"][:, 8 + pc:9 + pc],
                              ALU.mult, ALU.add),
                          waits=[ylt, seq_ab_tok, xnfree_tok] if pc == 0 else [],
                          inc="O" if pc == 7 else None)
        return (accb, parb, alt, yp_tok)

    load_x(0, 0)

    def direct_wmat(wnm, buf):
        toks = []
        for kc in range(8):
            toks.append(E.dma("gpsimd", "G", lambda e,
                              o=buf[:, 1024 * kc:1024 * (kc + 1)],
                              i=dram[wnm][128 * kc:128 * (kc + 1), :]:
                              e.dma_start(out=o, in_=i)))
        return toks
    wset_where["s_wq"] = (0, direct_wmat("wqT", wset[0]))
    pend_xn[(0, 0)] = run_ln(0, 0)
    wset_where["s_wk"] = (1, direct_wmat("wkT", wset[1]))
    wv0_tok = direct_wmat("wvT", sb["wdr"])

    def stage_rows(nm, snm, ntile, c0, c1, row0=0):
        toks = []
        for j in range(row0, row0 + ntile):
            toks.append(E.dma("gpsimd", "G", lambda e,
                              s=dram[nm][128 * j:128 * (j + 1), c0:c1],
                              d=dram[snm][128 * j:128 * (j + 1), c0:c1]:
                              e.dma_start(out=d, in_=s)))
        return toks
    stage_done[("s_wg", 0)] = stage_rows("wgT", "s_wg", 8, 0, 1024)
    stage_done[("s_wu", 0)] = stage_rows("wuT", "s_wu", 8, 0, 1024)
    stage_done[("s_wg", 1)] = stage_rows("wgT", "s_wg", 8, 1024, FF)
    stage_done[("s_wu", 1)] = stage_rows("wuT", "s_wu", 8, 1024, FF)
    for snm in ("s_wg", "s_wu"):
        stage_done[snm] = [stage_done[(snm, 0)], stage_done[(snm, 1)]]
    stage_done[("s_wd", 0)] = stage_rows("wdT", "s_wd", 8, 0, C, row0=0)
    stage_done["s_wq"] = stage_rows("wqT", "s_wq", 8, 0, C)
    stage_done["s_wk"] = stage_rows("wkT", "s_wk", 8, 0, C)
    stage_done[("s_wd", 1)] = stage_rows("wdT", "s_wd", 8, 0, C, row0=8)
    stage_done[("s_wd", 2)] = stage_rows("wdT", "s_wd", 8, 0, C, row0=16)
    stage_done["s_wv"] = stage_rows("wvT", "s_wv", 8, 0, C)
    stage_done["s_wo"] = stage_rows("woT", "s_wo", 8, 0, C)
    ha_free = None

    for seq in range(n_seq):
        gn_zero = E.op("vector", lambda e: e.memset(sb["gn_acc"][:], 0.0),
                       waits=[bankfree[3]], inc="V")

        for sc in range(n_sc):
            par = (seq * n_sc + sc) % 2
            xcur = xb[par]
            xt = x_tok[(seq, sc)]
            if sc + 1 < n_sc:
                load_x(seq, sc + 1)

            px = pend_xn.pop((seq, sc), None)
            if px is None:
                xn_g0, xn_ready = emit_xn_tp(run_ln(seq, sc))
            elif isinstance(px, tuple) and px and px[0] == "ready":
                xn_g0 = xn_ready = px[1]
            else:
                xn_g0, xn_ready = emit_xn_tp(px)

            load_win(0, win_free)
            load_win(1, win_free)

            mark(f"proj s{seq}c{sc}")
            slot_q, qtok = wset_where["s_wq"]
            slot_k, ktok = wset_where["s_wk"]
            qtev, ktev, vev = [], [], []

            q_done = None
            for m in range(8):
                bk = rot % 8
                rot += 1
                qlist = isinstance(qtok, list)
                w = [xn_g0, qtok[0] if qlist else qtok, bankfree[bk]]
                last = None
                for kc in range(8):
                    wk_ = w if kc == 0 else (
                        [xn_ready] if kc == 4 else [])
                    if kc > 0 and qlist:
                        wk_ = wk_ + [qtok[kc]]
                    last = mm(wk_, pb[bk][:],
                              wset[slot_q][:, 1024 * kc + 128 * m:
                                           1024 * kc + 128 * (m + 1)],
                              sb["xn"][:, 512 * kc:512 * (kc + 1)],
                              kc == 0, kc == 7, inc="P" if kc == 7 else None)
                q_done = last
                tq = E.op("vector", lambda e, b=pb[bk], m=m: e.tensor_mul(
                    sb["qt"][:, 512 * m:512 * (m + 1)], b[:],
                    sb["qdec"][:, 512 * m:512 * (m + 1)]),
                    waits=[last, ctok["qdec"]], inc="V")
                bankfree[bk] = tq
                qtev.append(tq)
            defer_wset = []
            if seq == 0 and sc == 0:
                defer_wset.append((slot_q, "s_wq", q_done))
            else:
                load_wset(slot_q, "s_wv", q_done)

            k_done = None
            for m in range(8):
                bk = rot % 8
                rot += 1
                klist = isinstance(ktok, list)
                w = [xn_ready, ktok[0] if klist else ktok, bankfree[bk]]
                last = None
                for kc in range(8):
                    wk_ = w if kc == 0 else (
                        [ktok[kc]] if klist else [])
                    last = mm(wk_, pb[bk][:],
                              wset[slot_k][:, 1024 * kc + 128 * m:
                                           1024 * kc + 128 * (m + 1)],
                              sb["xn"][:, 512 * kc:512 * (kc + 1)],
                              kc == 0, kc == 7, inc="P" if kc == 7 else None)
                k_done = last
                tk = E.op("vector", lambda e, b=pb[bk], m=m: e.tensor_mul(
                    sb["kt"][:, 512 * m:512 * (m + 1)], b[:],
                    sb["kdec"][:, 512 * m:512 * (m + 1)]),
                    waits=[last, ctok["kdec"]], inc="V")
                bankfree[bk] = tk
                ktev.append(tk)
            if seq == 0 and sc == 0:
                defer_wset.append((slot_k, "s_wk", k_done))
            elif sc + 1 < n_sc:
                load_wset(slot_k, "s_wq", k_done)
            else:
                load_wset(slot_k, "s_wo", k_done)

            if seq == 0 and sc == 0:
                slot_v, vtok, vsb = None, wv0_tok, sb["wdr"]
            else:
                slot_v, vtok = wset_where["s_wv"]
                vsb = wset[slot_v]
            v_done = None
            for mt in range(4):
                for ch in range(2):
                    bk = rot % 8
                    rot += 1
                    vlist = isinstance(vtok, list)
                    w = [vtok[0] if vlist else vtok, bankfree[bk]]
                    last = None
                    for kc in range(8):
                        wk_ = w if kc == 0 else (
                            [vtok[kc]] if vlist else [])
                        last = mm(wk_, pb[bk][:],
                                  sb["xn"][:, 512 * kc + 128 * mt:
                                           512 * kc + 128 * (mt + 1)],
                                  vsb[:, 1024 * kc + 512 * ch:
                                      1024 * kc + 512 * (ch + 1)],
                                  kc == 0, kc == 7,
                                  inc="P" if kc == 7 else None)
                    v_done = last
                    tv = E.op("scalar", lambda e, b=pb[bk], mt=mt, ch=ch:
                              e.activation(
                                  sb["vb"][:, 1024 * mt + 512 * ch:
                                           1024 * mt + 512 * (ch + 1)],
                                  b[:], AF.Copy), waits=[last], inc="A")
                    bankfree[bk] = tv
                    vev.append(tv)
            if seq == 0 and sc == 0:
                for s4 in range(4):
                    wd_free[s4] = v_done
            elif sc + 1 < n_sc:
                load_wset(slot_v, "s_wk", v_done)
            elif seq + 1 < n_seq:
                load_wset(slot_v, "s_wq", v_done)

            mark(f"ktm s{seq}c{sc}")
            ktm_last = None
            for mt in range(4):
                for g in range(2):
                    bk = rot % 8
                    rot += 1
                    last_tp = None
                    for j in range(4):
                        p = 4 * g + j
                        last_tp = tp([ktev[-1], bankfree[bk]] if j == 0
                                     else [],
                                     pb[bk][:, 128 * j:128 * (j + 1)],
                                     sb["kt"][:, 512 * p + 128 * mt:
                                              512 * p + 128 * (mt + 1)],
                                     inc="P" if j == 3 else None)
                    ktm_last = E.op("scalar", lambda e,
                                    o=sb["ha"][:, 1024 * mt + 512 * g:
                                               1024 * mt + 512 * (g + 1)],
                                    i=pb[bk][:]: e.activation(o, i, AF.Copy),
                                    waits=[last_tp, ha_free], inc="A")
                    bankfree[bk] = ktm_last

            mark(f"ret s{seq}c{sc}")
            sd_off = [0, 512, 896, 1152]
            sd_base = [4096, 4096 + 1280]
            sd_free = [None, None]
            y_free = [bankfree[7]]
            ret_rot = [0]
            yev_last = None
            s_all = {}
            sd_all = {}
            ydone = {}

            def emit_sbatch(pair, hh):
                part = 64 * hh
                for kc in range(NKC):
                    N = SCW - LK * kc
                    bk = ret_rot[0] % 6
                    ret_rot[0] += 1
                    s_all[(pair, hh, kc)] = (bk, N, mm(
                        [ktev[pair], qtev[pair], bankfree[bk]],
                        pb[bk][:, 0:N],
                        sb["kt"][part:part + 64,
                                 512 * pair + 128 * kc:
                                 512 * pair + 128 * (kc + 1)],
                        sb["qt"][part:part + 64,
                                 512 * pair + 128 * kc:
                                 512 * (pair + 1)],
                        True, True, inc="P"))

            def emit_sdevac(pair, hh):
                SPLIT = globals().get("SD_SPLIT", False)
                for kc in range(NKC):
                    bk, N, s_tok = s_all[(pair, hh, kc)]
                    base = sd_base[hh] + sd_off[kc]
                    if not SPLIT:
                        dg = E.op("vector", lambda e, b=pb[bk], N=N,
                                  o=sb["ha"][:, base:base + N]:
                                  e.tensor_mul(o, b[:, 0:N],
                                               sb["mask"][:, 0:N]),
                                  waits=[s_tok, sd_free[hh], ctok["mask"],
                                         ha_free], inc="V")
                        cp = None
                    else:
                        dg = E.op("vector", lambda e, b=pb[bk],
                                  o=sb["ha"][:, base:base + 128]:
                                  e.tensor_mul(o, b[:, 0:128],
                                               sb["mask"][:, 0:128]),
                                  waits=[s_tok, sd_free[hh], ctok["mask"],
                                         ha_free], inc="V")
                        if N > 128:
                            cp = E.op("scalar", lambda e, b=pb[bk], N=N,
                                      o=sb["ha"][:, base + 128:base + N]:
                                      e.activation(o, b[:, 128:N], AF.Copy),
                                      waits=[s_tok, sd_free[hh]], inc="A")
                        else:
                            cp = None
                    bankfree[bk] = [dg] if cp is None else [dg, cp]
                    sd_all[(pair, hh, kc)] = (dg, cp)

            def emit_ychain(pair, hh):
                h = 2 * pair + hh
                part = 64 * hh
                started = False
                if sc > 0:
                    mm([sbf_tok, qtev[pair], y_free[0]],
                       pb[7][part:part + 64, 0:512],
                       sb["Sbf"][part:part + 64, 64 * pair:64 * pair + 64],
                       sb["qt"][part:part + 64, 512 * pair:512 * (pair + 1)],
                       True, False)
                    started = True
                y_tok = None
                for kc in range(NKC):
                    N = SCW - LK * kc
                    dg, cp = sd_all[(pair, hh, kc)]
                    wy = [dg, cp, vev[-1]]
                    if not started:
                        wy.append(y_free[0])
                    y_tok = mm(wy, pb[7][part:part + 64, 128 * kc:512],
                               sb["vb"][:, 1024 * kc + 64 * h:
                                        1024 * kc + 64 * (h + 1)],
                               sb["ha"][:, sd_base[hh] + sd_off[kc]:
                                        sd_base[hh] + sd_off[kc] + N],
                               not started, kc == NKC - 1,
                               inc="P" if kc == NKC - 1 else None)
                    started = True
                sd_free[hh] = y_tok
                ydone[(pair, hh)] = y_tok

            PIPE = globals().get("RET_PIPE", False)
            if PIPE:
                emit_sbatch(0, 0)
                emit_sdevac(0, 0)
                emit_sbatch(0, 1)
            for pair in range(8):
                if PIPE:
                    emit_sdevac(pair, 1)
                    if pair + 1 < 8:
                        emit_sbatch(pair + 1, 0)
                    emit_ychain(pair, 0)
                    if pair + 1 < 8:
                        emit_sdevac(pair + 1, 0)
                        emit_sbatch(pair + 1, 1)
                    emit_ychain(pair, 1)
                else:
                    emit_sbatch(pair, 0)
                    emit_sdevac(pair, 0)
                    emit_sbatch(pair, 1)
                    emit_ychain(pair, 0)
                    emit_sdevac(pair, 1)
                    emit_ychain(pair, 1)
                y_tok = ydone[(pair, 1)]
                col = 16 * sc + 2 * pair
                jh = pair % 2
                a_cp = E.op("scalar", lambda e, p=pair: e.activation(
                    sb["yb"][:, 512 * p:512 * (p + 1)], pb[7][:], AF.Copy),
                    waits=[y_tok, ysb_free, gn_prev], inc="A")
                yev_last = E.op("scalar", lambda e, j=jh: e.activation(
                    sb["junk"][:, 512 * j:512 * (j + 1)], pb[7][:], AF.Square),
                    waits=[junk_sq_free[jh]], inc="A")
                y_free[0] = yev_last
                E.op("vector", lambda e, p=pair, c=col: e.reduce_sum(
                    sb["gn_acc"][:, c:c + 1], sb["yb"][:, 512 * p:512 * (p + 1)],
                    axis=mybir.AxisListType.X), waits=[a_cp, gn_zero])
                gnred = E.op("vector", lambda e, c=col, j=jh: e.reduce_sum(
                    sb["gn_acc"][:, c + 1:c + 2],
                    sb["junk"][:, 512 * j:512 * (j + 1)],
                    axis=mybir.AxisListType.X), waits=[yev_last], inc="V")
                junk_sq_free[jh] = gnred
            bankfree[7] = y_free[0]
            gn_last = gnred
            gn_prev = gn_last

            mark(f"state s{seq}c{sc}")
            st_tok = None
            w0 = [ktm_last, vev[-1], bankfree[6]]
            for pair in range(8):
                for hh in range(2):
                    h = 2 * pair + hh
                    for kc in range(NKC):
                        st_tok = mm(w0 if (pair == 0 and hh == 0 and kc == 0)
                                    else [],
                                    pb[6][64 * hh:64 * hh + 64,
                                          64 * pair:64 * pair + 64],
                                    sb["ha"][:, 1024 * kc + 64 * h:
                                             1024 * kc + 64 * (h + 1)],
                                    sb["vb"][:, 1024 * kc + 64 * h:
                                             1024 * kc + 64 * (h + 1)],
                                    kc == 0, kc == NKC - 1,
                                    inc="P" if (pair == 7 and hh == 1 and
                                                kc == NKC - 1) else None)
            if sc == 0:
                s1 = E.op("vector", lambda e: e.tensor_mul(
                    sb["Shat"][:], pb[6][:], sb["sdec"][:]),
                    waits=[st_tok, ctok["sdec"]], inc="V")
            else:
                s0 = E.op("vector", lambda e: e.tensor_add(
                    sb["Shat"][:], pb[6][:], sb["Shat"][:]),
                    waits=[st_tok, sbf_tok], inc="V")
                s1 = E.op("vector", lambda e: e.tensor_mul(
                    sb["Shat"][:], sb["Shat"][:], sb["sdec"][:]),
                    waits=[s0], inc="V")
            sbf_tok = E.op("vector", lambda e: e.tensor_copy(
                sb["Sbf"][:], sb["Shat"][:]), waits=[s1], inc="V")
            bankfree[6] = sbf_tok

            ysrc = sb["yb"][:].rearrange("p (pc t) -> p pc t", pc=8)
            ydst = dram["s_y"][seq, :, SCW * sc:SCW * (sc + 1)].rearrange(
                "(pc p) t -> p pc t", p=128)
            yscr_tok[(seq, sc)] = E.dma("gpsimd", "G",
                                        lambda e, o=ydst, i=ysrc:
                                        e.dma_start(out=o, in_=i),
                                        waits=[yev_last])
            ysb_free = yscr_tok[(seq, sc)]
            if sc == n_sc - 1:
                emit_gn(gn_last)
                st3 = st_tok

            mark(f"ffn s{seq}c{sc}")
            rot5 = rot
            wd_quads = {}

            def wd_slot_ap(s):
                if s < 4:
                    return sb["wdr"][:, 2048 * s:2048 * (s + 1)]
                if s < 6:
                    return sb["wgw"][:, 2048 * (s - 4):2048 * (s - 3)]
                return sb["wuw"][:, 2048 * (s - 6):2048 * (s - 5)]

            def load_quad(j):
                par_q = 0 if sc == 0 else wd_pair[0] % 2
                slot = (wd_pair[0] % 8)
                wd_pair[0] += 1
                jj = j % 12
                src_ap = dram["s_wd"][256 * jj:256 * (jj + 1), :].rearrange(
                    "(k p) c -> p k c", p=128)
                dst_ap = wd_slot_ap(slot).rearrange("p (k c) -> p k c", k=2)
                wt = E.dma("sync" if par_q == 0 else "gpsimd",
                           "W" if par_q == 0 else "G",
                           lambda e, o=dst_ap, i=src_ap:
                           e.dma_start(out=o, in_=i),
                           waits=[stage_done[("s_wd", jj // 4)],
                                  wd_free[slot]])
                wd_quads[j] = (slot, wt)
            wd_pair[0] += (-wd_pair[0]) % 8
            for jq in range(4):
                load_quad(jq)
            h_toks = [None] * NFT
            gu_last = None
            for w in range(NWIN):
                for m2 in range(2):
                    fm = 2 * w + m2
                    wslot = w % 2
                    bkg = rot5 % 8
                    rot5 += 1
                    wg = [win_tok[w][0], xn_ready, bankfree[bkg]]
                    g_last = None
                    for kc in range(8):
                        g_last = mm(wg if kc == 0 else [], pb[bkg][:],
                                    sb["wgw"][:, 2048 * wslot + 256 * kc +
                                              128 * m2:
                                              2048 * wslot + 256 * kc +
                                              128 * (m2 + 1)],
                                    sb["xn"][:, 512 * kc:512 * (kc + 1)],
                                    kc == 0, kc == 7,
                                    inc="P" if kc == 7 else None)
                    bku = rot5 % 8
                    rot5 += 1
                    wu = [win_tok[w][1], bankfree[bku]]
                    u_last = None
                    for kc in range(8):
                        u_last = mm(wu if kc == 0 else [], pb[bku][:],
                                    sb["wuw"][:, 2048 * wslot + 256 * kc +
                                              128 * m2:
                                              2048 * wslot + 256 * kc +
                                              128 * (m2 + 1)],
                                    sb["xn"][:, 512 * kc:512 * (kc + 1)],
                                    kc == 0, kc == 7,
                                    inc="P" if kc == 7 else None)
                    gu_last = u_last
                    sg = E.op("scalar", lambda e, b=pb[bkg], fm=fm:
                              e.activation(
                                  sb["silu2"][:, 512 * (fm % 2):
                                              512 * (fm % 2 + 1)],
                                  b[:], AF.Silu, scale=INV127),
                              waits=[g_last,
                                     h_toks[fm - 2] if fm >= 2 else None],
                              inc="A")
                    bankfree[bkg] = sg
                    wh = [u_last, sg, ha_free]
                    if fm < 8:
                        wh.append(st_tok)
                    elif fm < 13:
                        wh.append(sd_free[0])
                        wh.append(sd_free[1])
                    h_toks[fm] = E.op("vector", lambda e, b=pb[bku], fm=fm:
                                      e.scalar_tensor_tensor(
                                          sb["ha"][:, 512 * fm:512 * (fm + 1)],
                                          b[:], INV127,
                                          sb["silu2"][:, 512 * (fm % 2):
                                                      512 * (fm % 2 + 1)],
                                          ALU.mult, ALU.mult),
                                      waits=wh, inc="V")
                    bankfree[bku] = h_toks[fm]
                if w + 2 < NWIN:
                    load_win(w + 2, gu_last)
                if sc + 1 < n_sc:
                    if w == 0:
                        ln_pieces = emit_ln_chain(seq, sc + 1,
                                                  extra=[st_tok])
                    if w >= 1 and ln_pieces[0]:
                        ln_pieces[0].pop(0)()
                        if not ln_pieces[0]:
                            lnst_done(ln_pieces[1])
                            pend_xn[(seq, sc + 1)] = ln_pieces[1]["aff"]
            rot = rot5
            win_free = gu_last
            xn_free = gu_last
            if sc + 1 < n_sc:
                if ln_pieces[0]:
                    for p in ln_pieces[0]:
                        p()
                    lnst_done(ln_pieces[1])
                    pend_xn[(seq, sc + 1)] = ln_pieces[1]["aff"]
            else:
                pb_prep0 = emit_pb_prep(seq, 0, sb["xn"], sb["yb"],
                                        gu_last, [ysb_free, gn_last])

            for dslot, dsnm, dtok in defer_wset:
                load_wset(dslot, dsnm, dtok)

            mark(f"down s{seq}c{sc}")
            for s in range(4, 8):
                if wd_free[s] is None or True:
                    wd_free[s] = [wd_free[s], gu_last]
            fo_last = None
            down_tok = None
            regs8 = [(0, 0), (0, 1), (1, 0), (1, 1), (2, 0), (2, 1),
                     (3, 0), (3, 1)]
            for kc in range(24):
                qi = kc // 2
                q3, wt = wd_quads[qi]
                for ri, (mt, ch) in enumerate(regs8):
                    bk = ri
                    w = [wt, h_toks[kc]]
                    if kc == 0:
                        w.append(bankfree[bk])
                    down_tok = mm(w, pb[bk][:],
                                  sb["ha"][:, 512 * kc + 128 * mt:
                                           512 * kc + 128 * (mt + 1)],
                                  wd_slot_ap(q3)[:,
                                            1024 * (kc % 2) + 512 * ch:
                                            1024 * (kc % 2) + 512 * ch +
                                            512],
                                  kc == 0, kc == 23,
                                  inc="P" if ri == len(regs8) - 1
                                  else None)
                if kc % 2 == 1:
                    wd_free[q3] = down_tok
                    if qi + 4 < 12:
                        load_quad(qi + 4)
            for ri, (mt, ch) in enumerate(regs8):
                fo_last = E.op("vector", lambda e, b=pb[ri], xc=xcur,
                               mt=mt, ch=ch:
                               e.scalar_tensor_tensor(
                                   xc[:, 1024 * mt + 512 * ch:
                                      1024 * mt + 512 * (ch + 1)],
                                   b[:], INV127,
                                   xc[:, 1024 * mt + 512 * ch:
                                      1024 * mt + 512 * (ch + 1)],
                                   ALU.mult, ALU.add),
                               waits=[down_tok, xt], inc="V")
                bankfree[ri] = fo_last
            if "d_ha" in dram and seq == 0:
                down_tok = E.dma("gpsimd", "G",
                                 lambda e, o=dram["d_ha"][sc], i=sb["ha"][:]:
                                 e.dma_start(out=o, in_=i), waits=[down_tok])
            ha_free = down_tok
            win_free = down_tok

            asrc = xcur[:].rearrange("p (mt c) -> p mt c", mt=4)
            adst = dram["s_acc"][seq, SCW * sc:SCW * (sc + 1), :].rearrange(
                "(mt p) c -> p mt c", p=128)
            accscr_tok[(seq, sc)] = E.dma("gpsimd", "G",
                                          lambda e, o=adst, i=asrc:
                                          e.dma_start(out=o, in_=i),
                                          waits=[fo_last])
            xbuf_free[par] = accscr_tok[(seq, sc)]

        mark(f"passB s{seq}")
        lnp_next = None
        slot_wo, wo_tok = wset_where["s_wo"]
        prep = pb_prep0
        ao_hist = [st3, st3]
        yp_hist = [None, None]
        for scb in range(n_sc):
            accb, parb, alt, yp_tok = prep
            xnbuf = sb["xn"] if scb % 2 == 0 else sb["qt"]
            yp_hist[scb % 2] = yp_tok
            out_ev = None
            oev_mt = [None] * 4
            ao_last = None
            for mt in range(4):
                for ch in range(2):
                    bk = rot % 8
                    rot += 1
                    w = [yp_tok, wo_tok, bankfree[bk]]
                    last = None
                    for kc in range(8):
                        last = mm(w if kc == 0 else [], pb[bk][:],
                                  xnbuf[:, 512 * kc + 128 * mt:
                                        512 * kc + 128 * (mt + 1)],
                                  wset[slot_wo][:, 1024 * kc + 512 * ch:
                                               1024 * kc + 512 * (ch + 1)],
                                  kc == 0, kc == 7,
                                  inc="P" if kc == 7 else None)
                    ao_last = last
                    out_ev = E.op("vector", lambda e, b=pb[bk], ac=accb,
                                  mt=mt, ch=ch:
                                  e.tensor_add(
                                      ac[:, 1024 * mt + 512 * ch:
                                         1024 * mt + 512 * (ch + 1)],
                                      b[:],
                                      ac[:, 1024 * mt + 512 * ch:
                                         1024 * mt + 512 * (ch + 1)]),
                                  waits=[last, alt], inc="V")
                    oev_mt[mt] = out_ev
                    bankfree[bk] = out_ev
                    if lnp_next is not None and lnp_next[0]:
                        lnp_next[0].pop(0)()
                        if lnp_next[0]:
                            lnp_next[0].pop(0)()
            xn_free = ao_last
            if scb + 1 < n_sc:
                nb = scb + 1
                prep = emit_pb_prep(
                    seq, nb,
                    sb["xn"] if nb % 2 == 0 else sb["qt"],
                    sb["yb"] if nb % 2 == 0 else sb["kt"],
                    ao_hist[nb % 2],
                    [yp_hist[nb % 2], st3, yscr_tok[(seq, n_sc - 1)]])
            ao_hist[scb % 2] = ao_last
            ot = []
            for mt in range(4):
                ot.append(E.dma("gpsimd", "G", lambda e,
                                o=dram["out"][seq, SCW * scb + 128 * mt:
                                              SCW * scb + 128 * (mt + 1), :],
                                i=accb[:, 1024 * mt:1024 * (mt + 1)]:
                                e.dma_start(out=o, in_=i),
                                waits=[oev_mt[mt]]))
            out_toks.append(ot)
            xbuf_free[parb] = ot
            if scb == n_sc - 1 and lnp_next is not None:
                for p in lnp_next[0]:
                    p()
                lnst_done(lnp_next[1])
                pend_xn[(seq + 1, 0)] = lnp_next[1]["aff"]
            if scb == 2 and seq + 1 < n_seq:
                load_x(seq + 1, 0)
                lnp_next = emit_ln_chain(seq + 1, 0, extra=[st3])
        ysb_free = yp_hist[0]
        if seq + 1 < n_seq:
            load_wset(slot_wo, "s_wk", xn_free)

    E.wait("gpsimd", out_toks)


def kernel(L0_attn_x, L0_attn_wq, L0_attn_wk, L0_attn_wv, L0_attn_wo,
           L0_attn_gn_w, L0_attn_gn_b, L0_ffn_w_gate, L0_ffn_w_up,
           L0_ffn_w_down):
    x = np.ascontiguousarray(np.asarray(L0_attn_x))
    consts = make_consts()
    wqT = np.ascontiguousarray(np.asarray(L0_attn_wq).T)
    wkT = np.ascontiguousarray(np.asarray(L0_attn_wk).T)
    wvT = np.ascontiguousarray(np.asarray(L0_attn_wv).T)
    woT = np.ascontiguousarray(np.asarray(L0_attn_wo).T)
    wgT = np.ascontiguousarray(np.asarray(L0_ffn_w_gate).T)
    wuT = np.ascontiguousarray(np.asarray(L0_ffn_w_up).T)
    wdT = np.ascontiguousarray(np.asarray(L0_ffn_w_down).T)
    gnw = np.ascontiguousarray(np.asarray(L0_attn_gn_w).reshape(8, 128).T)
    gnb = np.ascontiguousarray(np.asarray(L0_attn_gn_b).reshape(8, 128).T)
    nc = build_nc()
    in_maps = []
    for core in range(NCORES):
        m = dict(x=np.ascontiguousarray(x[BSEQ * core:BSEQ * (core + 1)]),
                 wqT=wqT, wkT=wkT, wvT=wvT, woT=woT, wgT=wgT, wuT=wuT, wdT=wdT,
                 gnw=gnw, gnb=gnb)
        m.update(consts)
        in_maps.append(m)
    res = run_bass_kernel_spmd(nc, in_maps, core_ids=list(range(NCORES)))
    return np.concatenate([r["out"] for r in res.results], axis=0)


# version 34
# speedup vs baseline: 1.0756x; 1.0756x over previous
import numpy as np
import ml_dtypes
from contextlib import ExitStack
import concourse.bass as bass
import concourse.mybir as mybir
from concourse.bass_utils import run_bass_kernel_spmd

F32 = mybir.dt.float32
BF16 = mybir.dt.bfloat16
I32 = mybir.dt.int32
AF = mybir.ActivationFunctionType
ALU = mybir.AluOpType

B, T, C, H, D, FF = 32, 2048, 1024, 16, 64, 3072
NCORES = 8
BSEQ = B // NCORES
SCW = 512
NSC = T // SCW
LK = 128
NKC = SCW // LK
WF = 256
NWIN = FF // WF
NFT = FF // 128
LN_EPS = 1e-5
GN_EPS = 1e-5
INV127 = 1.0 / 127.0

ENGS = ("sync", "gpsimd", "scalar", "vector", "tensor")


def _gammas():
    lo = float(np.log(1.0 - 0.999))
    hi = float(np.log(1.0 - 0.9))
    lin = np.linspace(lo, hi, H, dtype=np.float32)
    return (1.0 - np.exp(lin.astype(np.float64)))


def make_consts():
    g = _gammas()
    i = np.arange(SCW, dtype=np.float64)
    qdec = np.zeros((128, 8 * SCW), dtype=np.float32)
    kdec = np.zeros((128, 8 * SCW), dtype=np.float32)
    for pc in range(8):
        for half in range(2):
            h = 2 * pc + half
            rows = slice(64 * half, 64 * half + 64)
            qdec[rows, SCW * pc:SCW * (pc + 1)] = (g[h] ** i).astype(np.float32)
            kdec[rows, SCW * pc:SCW * (pc + 1)] = (g[h] ** (-i)).astype(np.float32)
    mask = np.ones((128, SCW), dtype=np.float32)
    mask[:, :LK] = np.tril(np.ones((LK, LK), dtype=np.float32)).T
    sdec = np.zeros((128, 512), dtype=np.float32)
    for pair in range(8):
        for half in range(2):
            h = 2 * pair + half
            sdec[64 * half:64 * half + 64, 64 * pair:64 * pair + 64] = \
                np.float32(g[h] ** SCW)
    sel = np.zeros((128, 2), dtype=np.float32)
    sel[:64, 0] = 1.0
    sel[64:, 1] = 1.0
    selT = np.ascontiguousarray(sel.T)
    ident = np.eye(128, dtype=np.float32).astype(ml_dtypes.bfloat16)
    epsb = np.full((128, 1), LN_EPS, dtype=np.float32)
    return dict(qdec=qdec, kdec=kdec, mask=mask, sdec=sdec, sel=sel,
                selT=selT, ident=ident, epsb=epsb)


class Emit:

    def __init__(self):
        self.q = {e: [] for e in ENGS}
        self.qmark = {e: [] for e in ENGS}
        self.curmark = "start"
        self.cnt = {}
        self.handles = {}
        self.producer = {}
        self.seen = {e: {} for e in ENGS}

    def sem(self, name, handle, producer):
        self.handles[name] = handle
        self.cnt[name] = 0
        self.producer[name] = producer

    def pool(self, pname, lane_names):
        self.pools = getattr(self, "pools", {})
        self.pools[pname] = {"lanes": lane_names, "rr": 0,
                             "uses": [0] * len(lane_names)}

    def dma(self, eng, pname, fn, waits=()):
        st = self.pools[pname]
        i = st["rr"] % len(st["lanes"])
        st["rr"] += 1
        lane = st["lanes"][i]
        st["uses"][i] += 1
        u = st["uses"][i]
        w = list(waits) + [(lane, 16 * (u - 1))]
        return self.op(eng, fn, waits=w, inc=lane, inc_n=16)

    def tok(self, name):
        return (name, self.cnt[name])

    def _flatten(self, waits, out):
        for wv in waits:
            if wv is None:
                continue
            if isinstance(wv, list):
                self._flatten(wv, out)
            else:
                out.append(wv)
        return out

    def _waitlist(self, eng, waits):
        wl = []
        for wv in self._flatten(list(waits), []):
            s, v = wv
            if v <= 0 or (self.producer[s] == eng and eng == "tensor"):
                continue
            if self.seen[eng].get(s, -1) >= v:
                continue
            self.seen[eng][s] = v
            wl.append((self.handles[s], v))
        return wl

    def op(self, eng, fn, waits=(), inc=None, inc_n=1):
        wl = self._waitlist(eng, waits)
        tok = None
        h = n = None
        if inc is not None:
            self.cnt[inc] += inc_n
            tok = (inc, self.cnt[inc])
            h, n = self.handles[inc], inc_n

        def runner(e, wl=wl, fn=fn, h=h, n=n):
            for (sh, sv) in wl:
                e.wait_ge(sh, sv)
            inst = fn(e)
            if h is not None:
                inst.then_inc(h, n)
        self.q[eng].append(runner)
        self.qmark[eng].append(self.curmark)
        return tok

    def wait(self, eng, waits):
        wl = self._waitlist(eng, waits)
        if wl:
            def runner(e, wl=wl):
                for (sh, sv) in wl:
                    e.wait_ge(sh, sv)
            self.q[eng].append(runner)


def build_nc(n_seq=BSEQ, n_sc=NSC):
    nc = bass.Bass("TRN2", target_bir_lowering=False, debug=False)
    dram = {}
    dram["x"] = nc.dram_tensor("x", [BSEQ, T, C], F32, kind="ExternalInput").ap()
    for nm in ("wqT", "wkT", "wvT", "woT"):
        dram[nm] = nc.dram_tensor(nm, [C, C], F32, kind="ExternalInput").ap()
    for nm in ("wgT", "wuT"):
        dram[nm] = nc.dram_tensor(nm, [C, FF], I32, kind="ExternalInput").ap()
    dram["wdT"] = nc.dram_tensor("wdT", [FF, C], I32, kind="ExternalInput").ap()
    for nm, shp in (("gnw", [128, 8]), ("gnb", [128, 8]),
                    ("qdec", [128, 8 * SCW]), ("kdec", [128, 8 * SCW]),
                    ("mask", [128, SCW]), ("sdec", [128, 512]),
                    ("sel", [128, 2]), ("selT", [2, 128]),
                    ("epsb", [128, 1])):
        dram[nm] = nc.dram_tensor(nm, shp, F32, kind="ExternalInput").ap()
    dram["ident"] = nc.dram_tensor("ident", [128, 128], BF16,
                                   kind="ExternalInput").ap()
    dram["out"] = nc.dram_tensor("out", [BSEQ, T, C], F32,
                                 kind="ExternalOutput").ap()
    dbg = "ExternalOutput" if globals().get("DEBUG_SCRATCH") else "Internal"
    for nm, shp in (("s_wq", [C, C]), ("s_wk", [C, C]), ("s_wv", [C, C]),
                    ("s_wo", [C, C]), ("s_wg", [C, FF]), ("s_wu", [C, FF]),
                    ("s_wd", [FF, C])):
        dram[nm] = nc.dram_tensor(nm, shp, BF16, kind=dbg).ap()
    dram["s_y"] = nc.dram_tensor("s_y", [BSEQ, C, T], BF16, kind=dbg).ap()
    dram["s_acc"] = nc.dram_tensor("s_acc", [BSEQ, T, C], F32, kind=dbg).ap()
    if globals().get("DEBUG_SCRATCH"):
        dram["d_ha"] = nc.dram_tensor("d_ha", [NSC, 128, 12288], BF16,
                                      kind="ExternalOutput").ap()

    with ExitStack() as ctx:
        sb = {}

        def S(name, shape, dt):
            sb[name] = ctx.enter_context(nc.sbuf_tensor("sb_" + name, shape, dt))

        S("xb0", [128, 4096], F32)
        S("xb1", [128, 4096], F32)
        S("junk", [128, 1024], F32)
        S("lnst", [128, 16], F32)
        S("gn_acc", [128, 64], F32)
        S("gn_rows", [2, 64], F32)
        S("gn_der", [2, 48], F32)
        S("ab", [128, 16], F32)
        S("Shat", [128, 512], F32)
        for nm, shp in (("qdec", [128, 4096]), ("kdec", [128, 4096]),
                        ("mask", [128, SCW]), ("sdec", [128, 512]),
                        ("sel", [128, 2]), ("selT", [2, 128]),
                        ("gnw", [128, 8]), ("gnb", [128, 8]),
                        ("epsb", [128, 1])):
            S(nm, shp, F32)
        S("xn", [128, 4096], BF16)
        S("qt", [128, 4096], BF16)
        S("kt", [128, 4096], BF16)
        S("vb", [128, 4096], BF16)
        S("ha", [128, 12288], BF16)
        S("yb", [128, 4096], BF16)
        S("silu2", [128, 1024], BF16)
        S("Sbf", [128, 512], BF16)
        S("wset0", [128, 8192], BF16)
        S("wset1", [128, 8192], BF16)
        S("wgw", [128, 4096], BF16)
        S("wuw", [128, 4096], BF16)
        S("wdr", [128, 8192], BF16)
        S("ident", [128, 128], BF16)
        pb = [ctx.enter_context(nc.psum_tensor(f"pb{i}", [128, 512], F32))
              for i in range(8)]
        pbt = None

        E = Emit()
        for snm, prod in (("P", "tensor"), ("V", "vector"), ("A", "scalar"),
                          ("O", "gpsimd")):
            E.sem(snm, ctx.enter_context(nc.semaphore(snm)), prod)
        for pname, nlanes in (("W", 8), ("X", 4), ("G", 6)):
            lanes = []
            for i in range(nlanes):
                ln = f"{pname}{i}"
                E.sem(ln, ctx.enter_context(nc.semaphore(ln)), None)
                lanes.append(ln)
            E.pool(pname, lanes)

        emit_program(E, dram, sb, pb, pbt, n_seq, n_sc)
        build_nc.last_marks = (list(E.phasemarks), E.mmcount)
        build_nc.last_E = E

        with nc.Block() as block:
            @block.sync
            def _(e):
                for f in E.q["sync"]:
                    f(e)

            @block.gpsimd
            def _(e):
                for f in E.q["gpsimd"]:
                    f(e)

            @block.scalar
            def _(e):
                for f in E.q["scalar"]:
                    f(e)

            @block.vector
            def _(e):
                for f in E.q["vector"]:
                    f(e)

            @block.tensor
            def _(e):
                for f in E.q["tensor"]:
                    f(e)
    return nc


def emit_program(E, dram, sb, pb, pbt, n_seq, n_sc):
    ctok = {}
    for nm in ("ident", "epsb", "sel", "selT", "gnw", "gnb", "mask", "sdec",
               "qdec", "kdec"):
        ctok[nm] = E.dma("sync", "W", lambda e, nm=nm: e.dma_start(
            out=sb[nm][:], in_=dram[nm]))

    stage_done = {}

    xb = [sb["xb0"], sb["xb1"]]
    wset = [sb["wset0"], sb["wset1"]]
    x_tok = {}
    xbuf_free = [None, None]
    wset_where = {}
    bankfree = {i: None for i in range(8)}
    pbtfree = [None, None]
    win_tok = {}
    win_free = None
    wd_free = [None] * 8
    wd_pair = [0]
    yscr_tok = {}
    accscr_tok = {}
    out_toks = []
    sbf_tok = None
    xn_free = None
    ha_free = None
    ysb_free = None
    lnst_free = None
    junk_sq_free = [None, None]
    gn_prev = None

    E.mmcount = 0
    E.phasemarks = []

    def mark(name):
        E.phasemarks.append((E.mmcount, name))
        E.curmark = name

    def mm(waits, out, lhsT, rhs, start, stop, inc=None):
        E.mmcount += 1
        return E.op("tensor",
                    lambda e, o=out, l=lhsT, r=rhs, s1=start, s2=stop:
                    e.matmul(o, l, r, start=s1, stop=s2),
                    waits=waits, inc=inc)

    def tp(waits, out, in_, inc=None):
        E.mmcount += 1
        return E.op("tensor", lambda e, o=out, i=in_:
                    e.matmul(o, i, sb["ident"][:], start=True, stop=True),
                    waits=waits, inc=inc)

    def load_x(seq, sc):
        par = (seq * n_sc + sc) % 2
        free = xbuf_free[par]
        toks = []
        for mt in range(4):
            fr = free[mt] if (isinstance(free, list) and len(free) == 4) \
                else free
            toks.append(E.dma("scalar", "X", lambda e,
                              o=xb[par][:, 1024 * mt:1024 * (mt + 1)],
                              i=dram["x"][seq, SCW * sc + 128 * mt:
                                          SCW * sc + 128 * (mt + 1), :]:
                              e.dma_start(out=o, in_=i), waits=[fr]))
        x_tok[(seq, sc)] = toks

    def load_wset(slot, snm, free_tok):
        src = dram[snm][:, :].rearrange("(kc p) n -> p kc n", p=128)
        dst = wset[slot][:].rearrange("p (kc n) -> p kc n", kc=8)
        t = E.dma("sync", "W", lambda e, o=dst, i=src:
                  e.dma_start(out=o, in_=i),
                  waits=[stage_done[snm], free_tok])
        wset_where[snm] = (slot, t)
        return t

    def load_win(w, free_tok):
        toks = []
        for snm, buf in (("s_wg", "wgw"), ("s_wu", "wuw")):
            src = dram[snm][:, WF * w:WF * (w + 1)].rearrange(
                "(kc p) f -> p kc f", p=128)
            dst = sb[buf][:, 2048 * (w % 2):2048 * (w % 2 + 1)].rearrange(
                "p (kc f) -> p kc f", kc=8)
            toks.append(E.dma("sync", "W", lambda e, o=dst, i=src:
                              e.dma_start(out=o, in_=i),
                              waits=[stage_done[(snm, 0 if w < 4 else 1)],
                                     free_tok]))
        win_tok[w] = toks



    seq_ab_tok = None
    st3 = None
    rot = 0

    pend_xn = {}

    def emit_ln_chain(seq_i, sc_i, extra=()):
        st = {}
        xc = xb[(seq_i * n_sc + sc_i) % 2]
        xt_i = x_tok[(seq_i, sc_i)]

        def p_sum(a, b2):
            def f():
                for mt in (a, b2):
                    st["vsum"] = E.op("vector", lambda e, t=xc[
                        :, 1024 * mt:1024 * (mt + 1)], c=mt: e.reduce_sum(
                        sb["lnst"][:, c:c + 1], t, axis=mybir.AxisListType.X),
                        waits=[xt_i[mt], lnst_free], inc="V")
            return f

        def p_sq(mt):
            def f():
                a_sq = E.op("scalar", lambda e, t=xc[
                    :, 1024 * mt:1024 * (mt + 1)]: e.activation(
                    sb["junk"][:], t, AF.Square),
                    waits=[xt_i[mt], st.get("vsq"), junk_sq_free[0],
                           junk_sq_free[1]], inc="A")
                st["vsq"] = E.op("vector", lambda e, c=mt: e.reduce_sum(
                    sb["lnst"][:, 4 + c:5 + c], sb["junk"][:],
                    axis=mybir.AxisListType.X), waits=[a_sq], inc="V")
            return f

        def p_small():
            def f():
                nonlocal_state = None
                t1 = E.op("vector", lambda e: e.tensor_scalar(
                    sb["lnst"][:, 0:4], sb["lnst"][:, 0:4], 1.0 / C, 0.0,
                    ALU.mult, ALU.add), waits=[st["vsum"], st["vsq"]], inc="V")
                t2 = E.op("vector", lambda e: e.tensor_scalar(
                    sb["lnst"][:, 4:8], sb["lnst"][:, 4:8], 1.0 / C, 0.0,
                    ALU.mult, ALU.add), waits=[t1], inc="V")
                t3 = E.op("vector", lambda e: e.tensor_mul(
                    sb["lnst"][:, 8:12], sb["lnst"][:, 0:4],
                    sb["lnst"][:, 0:4]), waits=[t2], inc="V")
                v_var = E.op("vector", lambda e: e.tensor_sub(
                    sb["lnst"][:, 4:8], sb["lnst"][:, 4:8],
                    sb["lnst"][:, 8:12]), waits=[t3], inc="V")
                a_sqrt = E.op("scalar", lambda e: e.activation(
                    sb["lnst"][:, 8:12], sb["lnst"][:, 4:8], AF.Sqrt,
                    bias=sb["epsb"][:, 0:1]),
                    waits=[v_var, ctok["epsb"]], inc="A")
                t4 = E.op("vector", lambda e: e.reciprocal(
                    sb["lnst"][:, 8:12], sb["lnst"][:, 8:12]),
                    waits=[a_sqrt], inc="V")
                st["v_ln"] = E.op("vector", lambda e: e.scalar_tensor_tensor(
                    sb["lnst"][:, 12:16], sb["lnst"][:, 0:4], -1.0,
                    sb["lnst"][:, 8:12], ALU.mult, ALU.mult),
                    waits=[t4], inc="V")
            return f

        def p_aff(mt):
            def f():
                tmsl = sb["vb"][:, 1024 * mt:1024 * (mt + 1)]
                st.setdefault("aff", []).append(E.op(
                    "gpsimd", lambda e,
                    t=xc[:, 1024 * mt:1024 * (mt + 1)],
                    o=tmsl, c=mt: e.tensor_scalar(
                        o, t,
                        sb["lnst"][:, 8 + c:9 + c],
                        sb["lnst"][:, 12 + c:13 + c],
                        ALU.mult, ALU.add),
                    waits=[st["v_ln"], tp_done[0], tp_done[1], xt_i[mt]]
                    + list(extra), inc="O"))
            return f

        def finish():
            lnst_done(st)
            return st["aff"]
        pieces = [p_sum(0, 1), p_sum(2, 3), p_sq(0), p_sq(1), p_sq(2),
                  p_sq(3), p_small(), p_aff(0), p_aff(1), p_aff(2), p_aff(3)]
        return pieces, st

    def lnst_done(st):
        nonlocal lnst_free, junk_sq_free
        lnst_free = st["v_ln"]
        junk_sq_free = [st["vsq"], st["vsq"]]

    def run_ln(seq_i, sc_i, extra=()):
        pieces, st = emit_ln_chain(seq_i, sc_i, extra)
        for p in pieces:
            p()
        lnst_done(st)
        return st["aff"]

    tp_done = [None, None]

    def emit_xn_tp(aff_toks):
        nonlocal rot
        xn_g0 = xn_ready = None
        last_tp = None
        for g in range(2):
            for mt in range(4):
                tmsl = sb["vb"][:, 1024 * mt:1024 * (mt + 1)]
                bk = rot % 8
                rot += 1
                for j in range(4):
                    p = 4 * g + j
                    last_tp = tp([aff_toks[mt], ctok["ident"],
                                  bankfree[bk], xn_free] if j == 0 else [],
                                 pb[bk][:, 128 * j:128 * (j + 1)],
                                 tmsl[:, 128 * p:128 * (p + 1)],
                                 inc="P" if j == 3 else None)
                xo = sb["xn"][:].rearrange("p (k t) -> p k t", k=8)[
                    :, 4 * g:4 * g + 4, 128 * mt:128 * (mt + 1)]
                xi_ap = pb[bk][:].rearrange("p (j t) -> p j t", j=4)
                xn_ready = E.op("scalar", lambda e, o=xo, i=xi_ap:
                                e.activation(o, i, AF.Copy),
                                waits=[last_tp], inc="A")
                bankfree[bk] = xn_ready
            if g == 0:
                xn_g0 = xn_ready
        tp_done[0] = last_tp
        tp_done[1] = last_tp
        return xn_g0, xn_ready

    def emit_gn(gn_last_tok):
        nonlocal seq_ab_tok
        denom = 1.0 / float(n_sc * SCW * D)
        gst = mm([gn_last_tok, ctok["sel"], bankfree[3]],
                 pb[3][0:2, 0:64], sb["sel"][:], sb["gn_acc"][:],
                 True, True, inc="P")
        g0 = E.op("vector", lambda e: e.tensor_copy(
            sb["gn_rows"][:], pb[3][0:2, 0:64]), waits=[gst], inc="V")
        bankfree[3] = g0
        g1 = E.op("vector", lambda e: e.tensor_add(
            sb["gn_der"][:, 0:16], sb["gn_rows"][:, 0:16],
            sb["gn_rows"][:, 16:32]), waits=[g0], inc="V")
        g2 = E.op("vector", lambda e: e.tensor_add(
            sb["gn_der"][:, 0:16], sb["gn_der"][:, 0:16],
            sb["gn_rows"][:, 32:48]), waits=[g1], inc="V")
        g3 = E.op("vector", lambda e: e.tensor_add(
            sb["gn_der"][:, 0:16], sb["gn_der"][:, 0:16],
            sb["gn_rows"][:, 48:64]), waits=[g2], inc="V")
        gv = sb["gn_der"][:].rearrange("p (a b) -> p a b", b=2)
        g4 = E.op("vector", lambda e: e.tensor_scalar(
            sb["gn_der"][:, 16:24].rearrange("p (a b) -> p a b", b=1),
            gv[:, 0:8, 0:1], denom, 0.0, ALU.mult, ALU.add),
            waits=[g3], inc="V")
        g5 = E.op("vector", lambda e: e.tensor_scalar(
            sb["gn_der"][:, 24:32].rearrange("p (a b) -> p a b", b=1),
            gv[:, 0:8, 1:2], denom, 0.0, ALU.mult, ALU.add),
            waits=[g4], inc="V")
        g6 = E.op("vector", lambda e: e.tensor_mul(
            sb["gn_der"][:, 32:40], sb["gn_der"][:, 16:24],
            sb["gn_der"][:, 16:24]), waits=[g5], inc="V")
        vv = E.op("vector", lambda e: e.tensor_sub(
            sb["gn_der"][:, 24:32], sb["gn_der"][:, 24:32],
            sb["gn_der"][:, 32:40]), waits=[g6], inc="V")
        aa = E.op("scalar", lambda e: e.activation(
            sb["gn_der"][:, 32:40], sb["gn_der"][:, 24:32], AF.Sqrt,
            bias=sb["epsb"][0:2, 0:1]), waits=[vv], inc="A")
        g7 = E.op("vector", lambda e: e.reciprocal(
            sb["gn_der"][:, 32:40], sb["gn_der"][:, 32:40]), waits=[aa],
            inc="V")
        gder = E.op("vector", lambda e: e.tensor_mul(
            sb["gn_der"][:, 40:48], sb["gn_der"][:, 16:24],
            sb["gn_der"][:, 32:40]), waits=[g7], inc="V")
        mm([gder, ctok["selT"], bankfree[4]],
           pb[4][:, 0:8], sb["selT"][:], sb["gn_der"][:, 32:40], True, True)
        gbc = mm([], pb[4][:, 8:16], sb["selT"][:], sb["gn_der"][:, 40:48],
                 True, True, inc="P")
        b0 = E.op("vector", lambda e: e.tensor_mul(
            sb["ab"][:, 0:8], pb[4][:, 0:8], sb["gnw"][:]),
            waits=[gbc, ctok["gnw"], ctok["gnb"]], inc="V")
        b1 = E.op("vector", lambda e: e.scalar_tensor_tensor(
            sb["ab"][:, 8:16], pb[4][:, 8:16], -1.0, sb["gnw"][:],
            ALU.mult, ALU.mult), waits=[b0], inc="V")
        seq_ab_tok = E.op("vector", lambda e: e.tensor_add(
            sb["ab"][:, 8:16], sb["ab"][:, 8:16], sb["gnb"][:]),
            waits=[b1], inc="V")
        bankfree[4] = seq_ab_tok


    def emit_pb_prep(seq_i, scb, xnbuf, ybbuf, xnfree_tok, ybfree):
        parb = (seq_i * n_sc + scb) % 2
        accb = xb[parb]
        ysrc2 = dram["s_y"][seq_i, :, SCW * scb:SCW * (scb + 1)].rearrange(
            "(pc p) t -> p pc t", p=128)
        ydst2 = ybbuf[:].rearrange("p (pc t) -> p pc t", pc=8)
        ylt = E.dma("scalar", "X", lambda e, o=ydst2, i=ysrc2:
                    e.dma_start(out=o, in_=i),
                    waits=[yscr_tok[(seq_i, scb)]] + ybfree)
        asrc2 = dram["s_acc"][seq_i, SCW * scb:SCW * (scb + 1), :].rearrange(
            "(mt p) c -> p mt c", p=128)
        adst2 = accb[:].rearrange("p (mt c) -> p mt c", mt=4)
        alt = E.dma("scalar", "X", lambda e, o=adst2, i=asrc2:
                    e.dma_start(out=o, in_=i),
                    waits=[accscr_tok[(seq_i, scb)], xbuf_free[parb]])
        yp_tok = None
        for pc in range(8):
            yp_tok = E.op("gpsimd", lambda e, pc=pc, xnb=xnbuf, ybb=ybbuf:
                          e.tensor_scalar(
                              xnb[:, 512 * pc:512 * (pc + 1)],
                              ybb[:, 512 * pc:512 * (pc + 1)],
                              sb["ab"][:, pc:pc + 1],
                              sb["ab"][:, 8 + pc:9 + pc],
                              ALU.mult, ALU.add),
                          waits=[ylt, seq_ab_tok, xnfree_tok] if pc == 0 else [],
                          inc="O" if pc == 7 else None)
        return (accb, parb, alt, yp_tok)

    load_x(0, 0)

    def direct_wmat(wnm, buf):
        toks = []
        for kc in range(8):
            toks.append(E.dma("gpsimd", "G", lambda e,
                              o=buf[:, 1024 * kc:1024 * (kc + 1)],
                              i=dram[wnm][128 * kc:128 * (kc + 1), :]:
                              e.dma_start(out=o, in_=i)))
        return toks
    wset_where["s_wq"] = (0, direct_wmat("wqT", wset[0]))
    pend_xn[(0, 0)] = run_ln(0, 0)
    wset_where["s_wk"] = (1, direct_wmat("wkT", wset[1]))
    wv0_tok = direct_wmat("wvT", sb["wdr"])

    def stage_rows(nm, snm, ntile, c0, c1, row0=0):
        toks = []
        for j in range(row0, row0 + ntile):
            toks.append(E.dma("gpsimd", "G", lambda e,
                              s=dram[nm][128 * j:128 * (j + 1), c0:c1],
                              d=dram[snm][128 * j:128 * (j + 1), c0:c1]:
                              e.dma_start(out=d, in_=s)))
        return toks
    stage_done[("s_wg", 0)] = stage_rows("wgT", "s_wg", 8, 0, 1024)
    stage_done[("s_wu", 0)] = stage_rows("wuT", "s_wu", 8, 0, 1024)
    stage_done[("s_wg", 1)] = stage_rows("wgT", "s_wg", 8, 1024, FF)
    stage_done[("s_wu", 1)] = stage_rows("wuT", "s_wu", 8, 1024, FF)
    for snm in ("s_wg", "s_wu"):
        stage_done[snm] = [stage_done[(snm, 0)], stage_done[(snm, 1)]]
    stage_done[("s_wd", 0)] = stage_rows("wdT", "s_wd", 8, 0, C, row0=0)
    stage_done["s_wq"] = stage_rows("wqT", "s_wq", 8, 0, C)
    stage_done["s_wk"] = stage_rows("wkT", "s_wk", 8, 0, C)
    stage_done[("s_wd", 1)] = stage_rows("wdT", "s_wd", 8, 0, C, row0=8)
    stage_done[("s_wd", 2)] = stage_rows("wdT", "s_wd", 8, 0, C, row0=16)
    stage_done["s_wv"] = stage_rows("wvT", "s_wv", 8, 0, C)
    stage_done["s_wo"] = stage_rows("woT", "s_wo", 8, 0, C)
    ha_free = None

    for seq in range(n_seq):
        gn_zero = E.op("vector", lambda e: e.memset(sb["gn_acc"][:], 0.0),
                       waits=[bankfree[3]], inc="V")

        for sc in range(n_sc):
            par = (seq * n_sc + sc) % 2
            xcur = xb[par]
            xt = x_tok[(seq, sc)]
            if sc + 1 < n_sc:
                load_x(seq, sc + 1)

            px = pend_xn.pop((seq, sc), None)
            if px is None:
                xn_g0, xn_ready = emit_xn_tp(run_ln(seq, sc))
            elif isinstance(px, tuple) and px and px[0] == "ready":
                xn_g0 = xn_ready = px[1]
            else:
                xn_g0, xn_ready = emit_xn_tp(px)

            load_win(0, win_free)
            load_win(1, win_free)

            mark(f"proj s{seq}c{sc}")
            slot_q, qtok = wset_where["s_wq"]
            slot_k, ktok = wset_where["s_wk"]
            qtev, ktev, vev = [], [], []

            q_done = None
            for m in range(8):
                bk = rot % 8
                rot += 1
                w = [xn_g0, qtok, bankfree[bk]]
                last = None
                for kc in range(8):
                    last = mm(w if kc == 0 else
                              ([xn_ready] if kc == 4 else []), pb[bk][:],
                              wset[slot_q][:, 1024 * kc + 128 * m:
                                           1024 * kc + 128 * (m + 1)],
                              sb["xn"][:, 512 * kc:512 * (kc + 1)],
                              kc == 0, kc == 7, inc="P" if kc == 7 else None)
                q_done = last
                tq = E.op("vector", lambda e, b=pb[bk], m=m: e.tensor_mul(
                    sb["qt"][:, 512 * m:512 * (m + 1)], b[:],
                    sb["qdec"][:, 512 * m:512 * (m + 1)]),
                    waits=[last, ctok["qdec"]], inc="V")
                bankfree[bk] = tq
                qtev.append(tq)
            defer_wset = []
            if seq == 0 and sc == 0:
                defer_wset.append((slot_q, "s_wq", q_done))
            else:
                load_wset(slot_q, "s_wv", q_done)

            k_done = None
            for m in range(8):
                bk = rot % 8
                rot += 1
                w = [xn_ready, ktok, bankfree[bk]]
                last = None
                for kc in range(8):
                    last = mm(w if kc == 0 else [], pb[bk][:],
                              wset[slot_k][:, 1024 * kc + 128 * m:
                                           1024 * kc + 128 * (m + 1)],
                              sb["xn"][:, 512 * kc:512 * (kc + 1)],
                              kc == 0, kc == 7, inc="P" if kc == 7 else None)
                k_done = last
                tk = E.op("vector", lambda e, b=pb[bk], m=m: e.tensor_mul(
                    sb["kt"][:, 512 * m:512 * (m + 1)], b[:],
                    sb["kdec"][:, 512 * m:512 * (m + 1)]),
                    waits=[last, ctok["kdec"]], inc="V")
                bankfree[bk] = tk
                ktev.append(tk)
            if seq == 0 and sc == 0:
                defer_wset.append((slot_k, "s_wk", k_done))
            elif sc + 1 < n_sc:
                load_wset(slot_k, "s_wq", k_done)
            else:
                load_wset(slot_k, "s_wo", k_done)

            if seq == 0 and sc == 0:
                slot_v, vtok, vsb = None, wv0_tok, sb["wdr"]
            else:
                slot_v, vtok = wset_where["s_wv"]
                vsb = wset[slot_v]
            v_done = None
            for mt in range(4):
                for ch in range(2):
                    bk = rot % 8
                    rot += 1
                    w = [vtok, bankfree[bk]]
                    last = None
                    for kc in range(8):
                        last = mm(w if kc == 0 else [], pb[bk][:],
                                  sb["xn"][:, 512 * kc + 128 * mt:
                                           512 * kc + 128 * (mt + 1)],
                                  vsb[:, 1024 * kc + 512 * ch:
                                      1024 * kc + 512 * (ch + 1)],
                                  kc == 0, kc == 7,
                                  inc="P" if kc == 7 else None)
                    v_done = last
                    tv = E.op("scalar", lambda e, b=pb[bk], mt=mt, ch=ch:
                              e.activation(
                                  sb["vb"][:, 1024 * mt + 512 * ch:
                                           1024 * mt + 512 * (ch + 1)],
                                  b[:], AF.Copy), waits=[last], inc="A")
                    bankfree[bk] = tv
                    vev.append(tv)
            if seq == 0 and sc == 0:
                for s4 in range(4):
                    wd_free[s4] = v_done
            elif sc + 1 < n_sc:
                load_wset(slot_v, "s_wk", v_done)
            elif seq + 1 < n_seq:
                load_wset(slot_v, "s_wq", v_done)

            mark(f"ktm s{seq}c{sc}")
            ktm_last = None
            for mt in range(4):
                for g in range(2):
                    bk = rot % 8
                    rot += 1
                    last_tp = None
                    for j in range(4):
                        p = 4 * g + j
                        last_tp = tp([ktev[-1], bankfree[bk]] if j == 0
                                     else [],
                                     pb[bk][:, 128 * j:128 * (j + 1)],
                                     sb["kt"][:, 512 * p + 128 * mt:
                                              512 * p + 128 * (mt + 1)],
                                     inc="P" if j == 3 else None)
                    ktm_last = E.op("scalar", lambda e,
                                    o=sb["ha"][:, 1024 * mt + 512 * g:
                                               1024 * mt + 512 * (g + 1)],
                                    i=pb[bk][:]: e.activation(o, i, AF.Copy),
                                    waits=[last_tp, ha_free], inc="A")
                    bankfree[bk] = ktm_last

            mark(f"ret s{seq}c{sc}")
            sd_off = [0, 512, 896, 1152]
            sd_base = [4096, 4096 + 1280]
            sd_free = [None, None]
            y_free = [bankfree[7]]
            ret_rot = [0]
            yev_last = None
            s_all = {}
            sd_all = {}
            ydone = {}

            def emit_sbatch(pair, hh):
                part = 64 * hh
                for kc in range(NKC):
                    N = SCW - LK * kc
                    bk = ret_rot[0] % 7
                    ret_rot[0] += 1
                    s_all[(pair, hh, kc)] = (bk, N, mm(
                        [ktev[pair], qtev[pair], bankfree[bk]],
                        pb[bk][:, 0:N],
                        sb["kt"][part:part + 64,
                                 512 * pair + 128 * kc:
                                 512 * pair + 128 * (kc + 1)],
                        sb["qt"][part:part + 64,
                                 512 * pair + 128 * kc:
                                 512 * (pair + 1)],
                        True, True, inc="P"))

            def emit_sdevac(pair, hh):
                SPLIT = globals().get("SD_SPLIT", False)
                for kc in range(NKC):
                    bk, N, s_tok = s_all[(pair, hh, kc)]
                    base = sd_base[hh] + sd_off[kc]
                    if not SPLIT:
                        dg = E.op("vector", lambda e, b=pb[bk], N=N,
                                  o=sb["ha"][:, base:base + N]:
                                  e.tensor_mul(o, b[:, 0:N],
                                               sb["mask"][:, 0:N]),
                                  waits=[s_tok, sd_free[hh], ctok["mask"],
                                         ha_free], inc="V")
                        cp = None
                    else:
                        dg = E.op("vector", lambda e, b=pb[bk],
                                  o=sb["ha"][:, base:base + 128]:
                                  e.tensor_mul(o, b[:, 0:128],
                                               sb["mask"][:, 0:128]),
                                  waits=[s_tok, sd_free[hh], ctok["mask"],
                                         ha_free], inc="V")
                        if N > 128:
                            cp = E.op("scalar", lambda e, b=pb[bk], N=N,
                                      o=sb["ha"][:, base + 128:base + N]:
                                      e.activation(o, b[:, 128:N], AF.Copy),
                                      waits=[s_tok, sd_free[hh]], inc="A")
                        else:
                            cp = None
                    bankfree[bk] = [dg] if cp is None else [dg, cp]
                    sd_all[(pair, hh, kc)] = (dg, cp)

            def emit_ychain(pair, hh):
                h = 2 * pair + hh
                part = 64 * hh
                started = False
                if sc > 0:
                    mm([sbf_tok, qtev[pair], y_free[0]],
                       pb[7][part:part + 64, 0:512],
                       sb["Sbf"][part:part + 64, 64 * pair:64 * pair + 64],
                       sb["qt"][part:part + 64, 512 * pair:512 * (pair + 1)],
                       True, False)
                    started = True
                y_tok = None
                for kc in range(NKC):
                    N = SCW - LK * kc
                    dg, cp = sd_all[(pair, hh, kc)]
                    wy = [dg, cp, vev[-1]]
                    if not started:
                        wy.append(y_free[0])
                    y_tok = mm(wy, pb[7][part:part + 64, 128 * kc:512],
                               sb["vb"][:, 1024 * kc + 64 * h:
                                        1024 * kc + 64 * (h + 1)],
                               sb["ha"][:, sd_base[hh] + sd_off[kc]:
                                        sd_base[hh] + sd_off[kc] + N],
                               not started, kc == NKC - 1,
                               inc="P" if kc == NKC - 1 else None)
                    started = True
                sd_free[hh] = y_tok
                ydone[(pair, hh)] = y_tok

            PIPE = globals().get("RET_PIPE", False)
            if PIPE:
                emit_sbatch(0, 0)
                emit_sdevac(0, 0)
                emit_sbatch(0, 1)
            for pair in range(8):
                if PIPE:
                    emit_sdevac(pair, 1)
                    if pair + 1 < 8:
                        emit_sbatch(pair + 1, 0)
                    emit_ychain(pair, 0)
                    if pair + 1 < 8:
                        emit_sdevac(pair + 1, 0)
                        emit_sbatch(pair + 1, 1)
                    emit_ychain(pair, 1)
                else:
                    emit_sbatch(pair, 0)
                    emit_sdevac(pair, 0)
                    emit_sbatch(pair, 1)
                    emit_ychain(pair, 0)
                    emit_sdevac(pair, 1)
                    emit_ychain(pair, 1)
                y_tok = ydone[(pair, 1)]
                col = 16 * sc + 2 * pair
                jh = pair % 2
                a_cp = E.op("scalar", lambda e, p=pair: e.activation(
                    sb["yb"][:, 512 * p:512 * (p + 1)], pb[7][:], AF.Copy),
                    waits=[y_tok, ysb_free, gn_prev], inc="A")
                yev_last = E.op("scalar", lambda e, j=jh: e.activation(
                    sb["junk"][:, 512 * j:512 * (j + 1)], pb[7][:], AF.Square),
                    waits=[junk_sq_free[jh]], inc="A")
                y_free[0] = yev_last
                E.op("vector", lambda e, p=pair, c=col: e.reduce_sum(
                    sb["gn_acc"][:, c:c + 1], sb["yb"][:, 512 * p:512 * (p + 1)],
                    axis=mybir.AxisListType.X), waits=[a_cp, gn_zero])
                gnred = E.op("vector", lambda e, c=col, j=jh: e.reduce_sum(
                    sb["gn_acc"][:, c + 1:c + 2],
                    sb["junk"][:, 512 * j:512 * (j + 1)],
                    axis=mybir.AxisListType.X), waits=[yev_last], inc="V")
                junk_sq_free[jh] = gnred
            bankfree[7] = y_free[0]
            gn_last = gnred
            gn_prev = gn_last

            mark(f"state s{seq}c{sc}")
            st_tok = None
            w0 = [ktm_last, vev[-1], bankfree[6]]
            for pair in range(8):
                for hh in range(2):
                    h = 2 * pair + hh
                    for kc in range(NKC):
                        st_tok = mm(w0 if (pair == 0 and hh == 0 and kc == 0)
                                    else [],
                                    pb[6][64 * hh:64 * hh + 64,
                                          64 * pair:64 * pair + 64],
                                    sb["ha"][:, 1024 * kc + 64 * h:
                                             1024 * kc + 64 * (h + 1)],
                                    sb["vb"][:, 1024 * kc + 64 * h:
                                             1024 * kc + 64 * (h + 1)],
                                    kc == 0, kc == NKC - 1,
                                    inc="P" if (pair == 7 and hh == 1 and
                                                kc == NKC - 1) else None)
            if sc == 0:
                s1 = E.op("vector", lambda e: e.tensor_mul(
                    sb["Shat"][:], pb[6][:], sb["sdec"][:]),
                    waits=[st_tok, ctok["sdec"]], inc="V")
            else:
                s0 = E.op("vector", lambda e: e.tensor_add(
                    sb["Shat"][:], pb[6][:], sb["Shat"][:]),
                    waits=[st_tok, sbf_tok], inc="V")
                s1 = E.op("vector", lambda e: e.tensor_mul(
                    sb["Shat"][:], sb["Shat"][:], sb["sdec"][:]),
                    waits=[s0], inc="V")
            sbf_tok = E.op("vector", lambda e: e.tensor_copy(
                sb["Sbf"][:], sb["Shat"][:]), waits=[s1], inc="V")
            bankfree[6] = sbf_tok

            ysrc = sb["yb"][:].rearrange("p (pc t) -> p pc t", pc=8)
            ydst = dram["s_y"][seq, :, SCW * sc:SCW * (sc + 1)].rearrange(
                "(pc p) t -> p pc t", p=128)
            yscr_tok[(seq, sc)] = E.dma("gpsimd", "G",
                                        lambda e, o=ydst, i=ysrc:
                                        e.dma_start(out=o, in_=i),
                                        waits=[yev_last])
            ysb_free = yscr_tok[(seq, sc)]
            if sc == n_sc - 1:
                emit_gn(gn_last)
                st3 = st_tok

            mark(f"ffn s{seq}c{sc}")
            rot5 = rot
            wd_quads = {}

            def wd_slot_ap(s):
                if s < 4:
                    return sb["wdr"][:, 2048 * s:2048 * (s + 1)]
                if s < 6:
                    return sb["wgw"][:, 2048 * (s - 4):2048 * (s - 3)]
                return sb["wuw"][:, 2048 * (s - 6):2048 * (s - 5)]

            def load_quad(j):
                par_q = 0 if (seq == 0 and sc == 0) else wd_pair[0] % 2
                slot = (wd_pair[0] % 8)
                wd_pair[0] += 1
                jj = j % 12
                src_ap = dram["s_wd"][256 * jj:256 * (jj + 1), :].rearrange(
                    "(k p) c -> p k c", p=128)
                dst_ap = wd_slot_ap(slot).rearrange("p (k c) -> p k c", k=2)
                wt = E.dma("sync" if par_q == 0 else "gpsimd",
                           "W" if par_q == 0 else "G",
                           lambda e, o=dst_ap, i=src_ap:
                           e.dma_start(out=o, in_=i),
                           waits=[stage_done[("s_wd", jj // 4)],
                                  wd_free[slot]])
                wd_quads[j] = (slot, wt)
            wd_pair[0] += (-wd_pair[0]) % 8
            for jq in range(4):
                load_quad(jq)
            h_toks = [None] * NFT
            gu_last = None
            for w in range(NWIN):
                for m2 in range(2):
                    fm = 2 * w + m2
                    wslot = w % 2
                    bkg = rot5 % 8
                    rot5 += 1
                    wg = [win_tok[w][0], xn_ready, bankfree[bkg]]
                    g_last = None
                    for kc in range(8):
                        g_last = mm(wg if kc == 0 else [], pb[bkg][:],
                                    sb["wgw"][:, 2048 * wslot + 256 * kc +
                                              128 * m2:
                                              2048 * wslot + 256 * kc +
                                              128 * (m2 + 1)],
                                    sb["xn"][:, 512 * kc:512 * (kc + 1)],
                                    kc == 0, kc == 7,
                                    inc="P" if kc == 7 else None)
                    bku = rot5 % 8
                    rot5 += 1
                    wu = [win_tok[w][1], bankfree[bku]]
                    u_last = None
                    for kc in range(8):
                        u_last = mm(wu if kc == 0 else [], pb[bku][:],
                                    sb["wuw"][:, 2048 * wslot + 256 * kc +
                                              128 * m2:
                                              2048 * wslot + 256 * kc +
                                              128 * (m2 + 1)],
                                    sb["xn"][:, 512 * kc:512 * (kc + 1)],
                                    kc == 0, kc == 7,
                                    inc="P" if kc == 7 else None)
                    gu_last = u_last
                    sg = E.op("scalar", lambda e, b=pb[bkg], fm=fm:
                              e.activation(
                                  sb["silu2"][:, 512 * (fm % 2):
                                              512 * (fm % 2 + 1)],
                                  b[:], AF.Silu, scale=INV127),
                              waits=[g_last,
                                     h_toks[fm - 2] if fm >= 2 else None],
                              inc="A")
                    bankfree[bkg] = sg
                    wh = [u_last, sg, ha_free]
                    if fm < 8:
                        wh.append(st_tok)
                    elif fm < 13:
                        wh.append(sd_free[0])
                        wh.append(sd_free[1])
                    h_toks[fm] = E.op("vector", lambda e, b=pb[bku], fm=fm:
                                      e.scalar_tensor_tensor(
                                          sb["ha"][:, 512 * fm:512 * (fm + 1)],
                                          b[:], INV127,
                                          sb["silu2"][:, 512 * (fm % 2):
                                                      512 * (fm % 2 + 1)],
                                          ALU.mult, ALU.mult),
                                      waits=wh, inc="V")
                    bankfree[bku] = h_toks[fm]
                if w + 2 < NWIN:
                    load_win(w + 2, gu_last)
                if sc + 1 < n_sc:
                    if w == 0:
                        ln_pieces = emit_ln_chain(seq, sc + 1,
                                                  extra=[st_tok])
                    if w >= 1 and ln_pieces[0]:
                        ln_pieces[0].pop(0)()
                        if not ln_pieces[0]:
                            lnst_done(ln_pieces[1])
                            pend_xn[(seq, sc + 1)] = ln_pieces[1]["aff"]
            rot = rot5
            win_free = gu_last
            xn_free = gu_last
            if sc + 1 < n_sc:
                if ln_pieces[0]:
                    for p in ln_pieces[0]:
                        p()
                    lnst_done(ln_pieces[1])
                    pend_xn[(seq, sc + 1)] = ln_pieces[1]["aff"]
            else:
                pb_prep0 = emit_pb_prep(seq, 0, sb["xn"], sb["yb"],
                                        gu_last, [ysb_free, gn_last])

            for dslot, dsnm, dtok in defer_wset:
                load_wset(dslot, dsnm, dtok)

            mark(f"down s{seq}c{sc}")
            for s in range(4, 8):
                if wd_free[s] is None or True:
                    wd_free[s] = [wd_free[s], gu_last]
            fo_last = None
            down_tok = None
            regs8 = [(0, 0), (0, 1), (1, 0), (1, 1), (2, 0), (2, 1),
                     (3, 0), (3, 1)]
            for kc in range(24):
                qi = kc // 2
                q3, wt = wd_quads[qi]
                for ri, (mt, ch) in enumerate(regs8):
                    bk = ri
                    w = [wt, h_toks[kc]]
                    if kc == 0:
                        w.append(bankfree[bk])
                    down_tok = mm(w, pb[bk][:],
                                  sb["ha"][:, 512 * kc + 128 * mt:
                                           512 * kc + 128 * (mt + 1)],
                                  wd_slot_ap(q3)[:,
                                            1024 * (kc % 2) + 512 * ch:
                                            1024 * (kc % 2) + 512 * ch +
                                            512],
                                  kc == 0, kc == 23,
                                  inc="P" if ri == len(regs8) - 1
                                  else None)
                if kc % 2 == 1:
                    wd_free[q3] = down_tok
                    if qi + 4 < 12:
                        load_quad(qi + 4)
            for ri, (mt, ch) in enumerate(regs8):
                fo_last = E.op("vector", lambda e, b=pb[ri], xc=xcur,
                               mt=mt, ch=ch:
                               e.scalar_tensor_tensor(
                                   xc[:, 1024 * mt + 512 * ch:
                                      1024 * mt + 512 * (ch + 1)],
                                   b[:], INV127,
                                   xc[:, 1024 * mt + 512 * ch:
                                      1024 * mt + 512 * (ch + 1)],
                                   ALU.mult, ALU.add),
                               waits=[down_tok, xt], inc="V")
                bankfree[ri] = fo_last
            if "d_ha" in dram and seq == 0:
                down_tok = E.dma("gpsimd", "G",
                                 lambda e, o=dram["d_ha"][sc], i=sb["ha"][:]:
                                 e.dma_start(out=o, in_=i), waits=[down_tok])
            ha_free = down_tok
            win_free = down_tok

            asrc = xcur[:].rearrange("p (mt c) -> p mt c", mt=4)
            adst = dram["s_acc"][seq, SCW * sc:SCW * (sc + 1), :].rearrange(
                "(mt p) c -> p mt c", p=128)
            accscr_tok[(seq, sc)] = E.dma("gpsimd", "G",
                                          lambda e, o=adst, i=asrc:
                                          e.dma_start(out=o, in_=i),
                                          waits=[fo_last])
            xbuf_free[par] = accscr_tok[(seq, sc)]

        mark(f"passB s{seq}")
        lnp_next = None
        slot_wo, wo_tok = wset_where["s_wo"]
        prep = pb_prep0
        ao_hist = [st3, st3]
        yp_hist = [None, None]
        for scb in range(n_sc):
            accb, parb, alt, yp_tok = prep
            xnbuf = sb["xn"] if scb % 2 == 0 else sb["qt"]
            yp_hist[scb % 2] = yp_tok
            out_ev = None
            oev_mt = [None] * 4
            ao_last = None
            for mt in range(4):
                for ch in range(2):
                    bk = rot % 8
                    rot += 1
                    w = [yp_tok, wo_tok, bankfree[bk]]
                    last = None
                    for kc in range(8):
                        last = mm(w if kc == 0 else [], pb[bk][:],
                                  xnbuf[:, 512 * kc + 128 * mt:
                                        512 * kc + 128 * (mt + 1)],
                                  wset[slot_wo][:, 1024 * kc + 512 * ch:
                                               1024 * kc + 512 * (ch + 1)],
                                  kc == 0, kc == 7,
                                  inc="P" if kc == 7 else None)
                    ao_last = last
                    out_ev = E.op("vector", lambda e, b=pb[bk], ac=accb,
                                  mt=mt, ch=ch:
                                  e.tensor_add(
                                      ac[:, 1024 * mt + 512 * ch:
                                         1024 * mt + 512 * (ch + 1)],
                                      b[:],
                                      ac[:, 1024 * mt + 512 * ch:
                                         1024 * mt + 512 * (ch + 1)]),
                                  waits=[last, alt], inc="V")
                    oev_mt[mt] = out_ev
                    bankfree[bk] = out_ev
                    if lnp_next is not None and lnp_next[0]:
                        lnp_next[0].pop(0)()
                        if lnp_next[0]:
                            lnp_next[0].pop(0)()
            xn_free = ao_last
            if scb + 1 < n_sc:
                nb = scb + 1
                prep = emit_pb_prep(
                    seq, nb,
                    sb["xn"] if nb % 2 == 0 else sb["qt"],
                    sb["yb"] if nb % 2 == 0 else sb["kt"],
                    ao_hist[nb % 2],
                    [yp_hist[nb % 2], st3, yscr_tok[(seq, n_sc - 1)]])
            ao_hist[scb % 2] = ao_last
            ot = []
            for mt in range(4):
                ot.append(E.dma("gpsimd", "G", lambda e,
                                o=dram["out"][seq, SCW * scb + 128 * mt:
                                              SCW * scb + 128 * (mt + 1), :],
                                i=accb[:, 1024 * mt:1024 * (mt + 1)]:
                                e.dma_start(out=o, in_=i),
                                waits=[oev_mt[mt]]))
            out_toks.append(ot)
            xbuf_free[parb] = ot
            if scb == n_sc - 1 and lnp_next is not None:
                for p in lnp_next[0]:
                    p()
                lnst_done(lnp_next[1])
                pend_xn[(seq + 1, 0)] = lnp_next[1]["aff"]
            if scb == 2 and seq + 1 < n_seq:
                load_x(seq + 1, 0)
                lnp_next = emit_ln_chain(seq + 1, 0, extra=[st3])
        ysb_free = yp_hist[0]
        if seq + 1 < n_seq:
            load_wset(slot_wo, "s_wk", xn_free)

    E.wait("gpsimd", out_toks)


def kernel(L0_attn_x, L0_attn_wq, L0_attn_wk, L0_attn_wv, L0_attn_wo,
           L0_attn_gn_w, L0_attn_gn_b, L0_ffn_w_gate, L0_ffn_w_up,
           L0_ffn_w_down):
    x = np.ascontiguousarray(np.asarray(L0_attn_x))
    consts = make_consts()
    wqT = np.ascontiguousarray(np.asarray(L0_attn_wq).T)
    wkT = np.ascontiguousarray(np.asarray(L0_attn_wk).T)
    wvT = np.ascontiguousarray(np.asarray(L0_attn_wv).T)
    woT = np.ascontiguousarray(np.asarray(L0_attn_wo).T)
    wgT = np.ascontiguousarray(np.asarray(L0_ffn_w_gate).T)
    wuT = np.ascontiguousarray(np.asarray(L0_ffn_w_up).T)
    wdT = np.ascontiguousarray(np.asarray(L0_ffn_w_down).T)
    gnw = np.ascontiguousarray(np.asarray(L0_attn_gn_w).reshape(8, 128).T)
    gnb = np.ascontiguousarray(np.asarray(L0_attn_gn_b).reshape(8, 128).T)
    nc = build_nc()
    in_maps = []
    for core in range(NCORES):
        m = dict(x=np.ascontiguousarray(x[BSEQ * core:BSEQ * (core + 1)]),
                 wqT=wqT, wkT=wkT, wvT=wvT, woT=woT, wgT=wgT, wuT=wuT, wdT=wdT,
                 gnw=gnw, gnb=gnb)
        m.update(consts)
        in_maps.append(m)
    res = run_bass_kernel_spmd(nc, in_maps, core_ids=list(range(NCORES)))
    return np.concatenate([r["out"] for r in res.results], axis=0)


# version 35
# speedup vs baseline: 1.1136x; 1.0354x over previous
import numpy as np
import ml_dtypes
from contextlib import ExitStack
import concourse.bass as bass
import concourse.mybir as mybir
from concourse.bass_utils import run_bass_kernel_spmd

F32 = mybir.dt.float32
BF16 = mybir.dt.bfloat16
I32 = mybir.dt.int32
AF = mybir.ActivationFunctionType
ALU = mybir.AluOpType

B, T, C, H, D, FF = 32, 2048, 1024, 16, 64, 3072
NCORES = 8
BSEQ = B // NCORES
SCW = 512
NSC = T // SCW
LK = 128
NKC = SCW // LK
WF = 256
NWIN = FF // WF
NFT = FF // 128
LN_EPS = 1e-5
GN_EPS = 1e-5
INV127 = 1.0 / 127.0

ENGS = ("sync", "gpsimd", "scalar", "vector", "tensor")


def _gammas():
    lo = float(np.log(1.0 - 0.999))
    hi = float(np.log(1.0 - 0.9))
    lin = np.linspace(lo, hi, H, dtype=np.float32)
    return (1.0 - np.exp(lin.astype(np.float64)))


def make_consts():
    g = _gammas()
    i = np.arange(SCW, dtype=np.float64)
    qdec = np.zeros((128, 8 * SCW), dtype=np.float32)
    kdec = np.zeros((128, 8 * SCW), dtype=np.float32)
    for pc in range(8):
        for half in range(2):
            h = 2 * pc + half
            rows = slice(64 * half, 64 * half + 64)
            qdec[rows, SCW * pc:SCW * (pc + 1)] = (g[h] ** i).astype(np.float32)
            kdec[rows, SCW * pc:SCW * (pc + 1)] = (g[h] ** (-i)).astype(np.float32)
    mask = np.ones((128, SCW), dtype=np.float32)
    mask[:, :LK] = np.tril(np.ones((LK, LK), dtype=np.float32)).T
    sdec = np.zeros((128, 512), dtype=np.float32)
    for pair in range(8):
        for half in range(2):
            h = 2 * pair + half
            sdec[64 * half:64 * half + 64, 64 * pair:64 * pair + 64] = \
                np.float32(g[h] ** SCW)
    sel = np.zeros((128, 2), dtype=np.float32)
    sel[:64, 0] = 1.0
    sel[64:, 1] = 1.0
    selT = np.ascontiguousarray(sel.T)
    ident = np.eye(128, dtype=np.float32).astype(ml_dtypes.bfloat16)
    epsb = np.full((128, 1), LN_EPS, dtype=np.float32)
    return dict(qdec=qdec, kdec=kdec, mask=mask, sdec=sdec, sel=sel,
                selT=selT, ident=ident, epsb=epsb)


class Emit:

    def __init__(self):
        self.q = {e: [] for e in ENGS}
        self.qmark = {e: [] for e in ENGS}
        self.curmark = "start"
        self.cnt = {}
        self.handles = {}
        self.producer = {}
        self.seen = {e: {} for e in ENGS}

    def sem(self, name, handle, producer):
        self.handles[name] = handle
        self.cnt[name] = 0
        self.producer[name] = producer

    def pool(self, pname, lane_names):
        self.pools = getattr(self, "pools", {})
        self.pools[pname] = {"lanes": lane_names, "rr": 0,
                             "uses": [0] * len(lane_names)}

    def dma(self, eng, pname, fn, waits=()):
        st = self.pools[pname]
        i = st["rr"] % len(st["lanes"])
        st["rr"] += 1
        lane = st["lanes"][i]
        st["uses"][i] += 1
        u = st["uses"][i]
        w = list(waits) + [(lane, 16 * (u - 1))]
        return self.op(eng, fn, waits=w, inc=lane, inc_n=16)

    def tok(self, name):
        return (name, self.cnt[name])

    def _flatten(self, waits, out):
        for wv in waits:
            if wv is None:
                continue
            if isinstance(wv, list):
                self._flatten(wv, out)
            else:
                out.append(wv)
        return out

    def _waitlist(self, eng, waits):
        wl = []
        for wv in self._flatten(list(waits), []):
            s, v = wv
            if v <= 0 or (self.producer[s] == eng and eng == "tensor"):
                continue
            if self.seen[eng].get(s, -1) >= v:
                continue
            self.seen[eng][s] = v
            wl.append((self.handles[s], v))
        return wl

    def op(self, eng, fn, waits=(), inc=None, inc_n=1):
        wl = self._waitlist(eng, waits)
        tok = None
        h = n = None
        if inc is not None:
            self.cnt[inc] += inc_n
            tok = (inc, self.cnt[inc])
            h, n = self.handles[inc], inc_n

        def runner(e, wl=wl, fn=fn, h=h, n=n):
            for (sh, sv) in wl:
                e.wait_ge(sh, sv)
            inst = fn(e)
            if h is not None:
                inst.then_inc(h, n)
        self.q[eng].append(runner)
        self.qmark[eng].append(self.curmark)
        return tok

    def wait(self, eng, waits):
        wl = self._waitlist(eng, waits)
        if wl:
            def runner(e, wl=wl):
                for (sh, sv) in wl:
                    e.wait_ge(sh, sv)
            self.q[eng].append(runner)


def build_nc(n_seq=BSEQ, n_sc=NSC):
    nc = bass.Bass("TRN2", target_bir_lowering=False, debug=False)
    dram = {}
    dram["x"] = nc.dram_tensor("x", [BSEQ, T, C], F32, kind="ExternalInput").ap()
    for nm in ("wqT", "wkT", "wvT", "woT"):
        dram[nm] = nc.dram_tensor(nm, [C, C], F32, kind="ExternalInput").ap()
    for nm in ("wgT", "wuT"):
        dram[nm] = nc.dram_tensor(nm, [C, FF], I32, kind="ExternalInput").ap()
    dram["wdT"] = nc.dram_tensor("wdT", [FF, C], I32, kind="ExternalInput").ap()
    for nm, shp in (("gnw", [128, 8]), ("gnb", [128, 8]),
                    ("qdec", [128, 8 * SCW]), ("kdec", [128, 8 * SCW]),
                    ("mask", [128, SCW]), ("sdec", [128, 512]),
                    ("sel", [128, 2]), ("selT", [2, 128]),
                    ("epsb", [128, 1])):
        dram[nm] = nc.dram_tensor(nm, shp, F32, kind="ExternalInput").ap()
    dram["ident"] = nc.dram_tensor("ident", [128, 128], BF16,
                                   kind="ExternalInput").ap()
    dram["out"] = nc.dram_tensor("out", [BSEQ, T, C], F32,
                                 kind="ExternalOutput").ap()
    dbg = "ExternalOutput" if globals().get("DEBUG_SCRATCH") else "Internal"
    for nm, shp in (("s_wq", [C, C]), ("s_wk", [C, C]), ("s_wv", [C, C]),
                    ("s_wo", [C, C]), ("s_wg", [C, FF]), ("s_wu", [C, FF]),
                    ("s_wd", [FF, C])):
        dram[nm] = nc.dram_tensor(nm, shp, BF16, kind=dbg).ap()
    dram["s_y"] = nc.dram_tensor("s_y", [BSEQ, C, T], BF16, kind=dbg).ap()
    dram["s_acc"] = nc.dram_tensor("s_acc", [BSEQ, T, C], F32, kind=dbg).ap()
    if globals().get("DEBUG_SCRATCH"):
        dram["d_ha"] = nc.dram_tensor("d_ha", [NSC, 128, 12288], BF16,
                                      kind="ExternalOutput").ap()

    with ExitStack() as ctx:
        sb = {}

        def S(name, shape, dt):
            sb[name] = ctx.enter_context(nc.sbuf_tensor("sb_" + name, shape, dt))

        S("xb0", [128, 4096], F32)
        S("xb1", [128, 4096], F32)
        S("junk", [128, 1024], F32)
        S("lnst", [128, 16], F32)
        S("gn_acc", [128, 64], F32)
        S("gn_rows", [2, 64], F32)
        S("gn_der", [2, 48], F32)
        S("ab", [128, 16], F32)
        S("Shat", [128, 512], F32)
        for nm, shp in (("qdec", [128, 4096]), ("kdec", [128, 4096]),
                        ("mask", [128, SCW]), ("sdec", [128, 512]),
                        ("sel", [128, 2]), ("selT", [2, 128]),
                        ("gnw", [128, 8]), ("gnb", [128, 8]),
                        ("epsb", [128, 1])):
            S(nm, shp, F32)
        S("xn", [128, 4096], BF16)
        S("qt", [128, 4096], BF16)
        S("kt", [128, 4096], BF16)
        S("vb", [128, 4096], BF16)
        S("ha", [128, 12288], BF16)
        S("yb", [128, 4096], BF16)
        S("silu2", [128, 1024], BF16)
        S("Sbf", [128, 512], BF16)
        S("wset0", [128, 8192], BF16)
        S("wset1", [128, 8192], BF16)
        S("wgw", [128, 4096], BF16)
        S("wuw", [128, 4096], BF16)
        S("wdr", [128, 8192], BF16)
        S("ident", [128, 128], BF16)
        pb = [ctx.enter_context(nc.psum_tensor(f"pb{i}", [128, 512], F32))
              for i in range(8)]
        pbt = None

        E = Emit()
        for snm, prod in (("P", "tensor"), ("V", "vector"), ("A", "scalar"),
                          ("O", "gpsimd")):
            E.sem(snm, ctx.enter_context(nc.semaphore(snm)), prod)
        for pname, nlanes in (("W", 8), ("X", 6), ("G", 8)):
            lanes = []
            for i in range(nlanes):
                ln = f"{pname}{i}"
                E.sem(ln, ctx.enter_context(nc.semaphore(ln)), None)
                lanes.append(ln)
            E.pool(pname, lanes)

        emit_program(E, dram, sb, pb, pbt, n_seq, n_sc)
        build_nc.last_marks = (list(E.phasemarks), E.mmcount)
        build_nc.last_E = E

        with nc.Block() as block:
            @block.sync
            def _(e):
                for f in E.q["sync"]:
                    f(e)

            @block.gpsimd
            def _(e):
                for f in E.q["gpsimd"]:
                    f(e)

            @block.scalar
            def _(e):
                for f in E.q["scalar"]:
                    f(e)

            @block.vector
            def _(e):
                for f in E.q["vector"]:
                    f(e)

            @block.tensor
            def _(e):
                for f in E.q["tensor"]:
                    f(e)
    return nc


def emit_program(E, dram, sb, pb, pbt, n_seq, n_sc):
    ctok = {}
    for nm in ("ident", "epsb", "sel", "selT", "gnw", "gnb", "mask", "sdec",
               "qdec", "kdec"):
        ctok[nm] = E.dma("sync", "W", lambda e, nm=nm: e.dma_start(
            out=sb[nm][:], in_=dram[nm]))

    stage_done = {}

    xb = [sb["xb0"], sb["xb1"]]
    wset = [sb["wset0"], sb["wset1"]]
    x_tok = {}
    xbuf_free = [None, None]
    wset_where = {}
    bankfree = {i: None for i in range(8)}
    pbtfree = [None, None]
    win_tok = {}
    win_free = None
    wd_free = [None] * 8
    wd_pair = [0]
    yscr_tok = {}
    accscr_tok = {}
    out_toks = []
    sbf_tok = None
    xn_free = None
    ha_free = None
    ysb_free = None
    lnst_free = None
    junk_sq_free = [None, None]
    gn_prev = None

    E.mmcount = 0
    E.phasemarks = []

    def mark(name):
        E.phasemarks.append((E.mmcount, name))
        E.curmark = name

    def mm(waits, out, lhsT, rhs, start, stop, inc=None):
        E.mmcount += 1
        return E.op("tensor",
                    lambda e, o=out, l=lhsT, r=rhs, s1=start, s2=stop:
                    e.matmul(o, l, r, start=s1, stop=s2),
                    waits=waits, inc=inc)

    def tp(waits, out, in_, inc=None):
        E.mmcount += 1
        return E.op("tensor", lambda e, o=out, i=in_:
                    e.matmul(o, i, sb["ident"][:], start=True, stop=True),
                    waits=waits, inc=inc)

    def load_x(seq, sc):
        par = (seq * n_sc + sc) % 2
        free = xbuf_free[par]
        toks = []
        for mt in range(4):
            fr = free[mt] if (isinstance(free, list) and len(free) == 4) \
                else free
            toks.append(E.dma("scalar", "X", lambda e,
                              o=xb[par][:, 1024 * mt:1024 * (mt + 1)],
                              i=dram["x"][seq, SCW * sc + 128 * mt:
                                          SCW * sc + 128 * (mt + 1), :]:
                              e.dma_start(out=o, in_=i), waits=[fr]))
        x_tok[(seq, sc)] = toks

    def load_wset(slot, snm, free_tok):
        src = dram[snm][:, :].rearrange("(kc p) n -> p kc n", p=128)
        dst = wset[slot][:].rearrange("p (kc n) -> p kc n", kc=8)
        t = E.dma("sync", "W", lambda e, o=dst, i=src:
                  e.dma_start(out=o, in_=i),
                  waits=[stage_done[snm], free_tok])
        wset_where[snm] = (slot, t)
        return t

    def load_win(w, free_tok):
        toks = []
        for snm, buf in (("s_wg", "wgw"), ("s_wu", "wuw")):
            src = dram[snm][:, WF * w:WF * (w + 1)].rearrange(
                "(kc p) f -> p kc f", p=128)
            dst = sb[buf][:, 2048 * (w % 2):2048 * (w % 2 + 1)].rearrange(
                "p (kc f) -> p kc f", kc=8)
            toks.append(E.dma("sync", "W", lambda e, o=dst, i=src:
                              e.dma_start(out=o, in_=i),
                              waits=[stage_done[(snm, 0 if w < 4 else 1)],
                                     free_tok]))
        win_tok[w] = toks



    seq_ab_tok = None
    st3 = None
    rot = 0

    pend_xn = {}

    def emit_ln_chain(seq_i, sc_i, extra=()):
        st = {}
        xc = xb[(seq_i * n_sc + sc_i) % 2]
        xt_i = x_tok[(seq_i, sc_i)]

        def p_sum(a, b2):
            def f():
                for mt in (a, b2):
                    st["vsum"] = E.op("vector", lambda e, t=xc[
                        :, 1024 * mt:1024 * (mt + 1)], c=mt: e.reduce_sum(
                        sb["lnst"][:, c:c + 1], t, axis=mybir.AxisListType.X),
                        waits=[xt_i[mt], lnst_free], inc="V")
            return f

        def p_sq(mt):
            def f():
                a_sq = E.op("scalar", lambda e, t=xc[
                    :, 1024 * mt:1024 * (mt + 1)]: e.activation(
                    sb["junk"][:], t, AF.Square),
                    waits=[xt_i[mt], st.get("vsq"), junk_sq_free[0],
                           junk_sq_free[1]], inc="A")
                st["vsq"] = E.op("vector", lambda e, c=mt: e.reduce_sum(
                    sb["lnst"][:, 4 + c:5 + c], sb["junk"][:],
                    axis=mybir.AxisListType.X), waits=[a_sq], inc="V")
            return f

        def p_small():
            def f():
                nonlocal_state = None
                t1 = E.op("vector", lambda e: e.tensor_scalar(
                    sb["lnst"][:, 0:4], sb["lnst"][:, 0:4], 1.0 / C, 0.0,
                    ALU.mult, ALU.add), waits=[st["vsum"], st["vsq"]], inc="V")
                t2 = E.op("vector", lambda e: e.tensor_scalar(
                    sb["lnst"][:, 4:8], sb["lnst"][:, 4:8], 1.0 / C, 0.0,
                    ALU.mult, ALU.add), waits=[t1], inc="V")
                t3 = E.op("vector", lambda e: e.tensor_mul(
                    sb["lnst"][:, 8:12], sb["lnst"][:, 0:4],
                    sb["lnst"][:, 0:4]), waits=[t2], inc="V")
                v_var = E.op("vector", lambda e: e.tensor_sub(
                    sb["lnst"][:, 4:8], sb["lnst"][:, 4:8],
                    sb["lnst"][:, 8:12]), waits=[t3], inc="V")
                a_sqrt = E.op("scalar", lambda e: e.activation(
                    sb["lnst"][:, 8:12], sb["lnst"][:, 4:8], AF.Sqrt,
                    bias=sb["epsb"][:, 0:1]),
                    waits=[v_var, ctok["epsb"]], inc="A")
                t4 = E.op("vector", lambda e: e.reciprocal(
                    sb["lnst"][:, 8:12], sb["lnst"][:, 8:12]),
                    waits=[a_sqrt], inc="V")
                st["v_ln"] = E.op("vector", lambda e: e.scalar_tensor_tensor(
                    sb["lnst"][:, 12:16], sb["lnst"][:, 0:4], -1.0,
                    sb["lnst"][:, 8:12], ALU.mult, ALU.mult),
                    waits=[t4], inc="V")
            return f

        def p_aff(mt):
            def f():
                tmsl = sb["vb"][:, 1024 * mt:1024 * (mt + 1)]
                st.setdefault("aff", []).append(E.op(
                    "gpsimd", lambda e,
                    t=xc[:, 1024 * mt:1024 * (mt + 1)],
                    o=tmsl, c=mt: e.tensor_scalar(
                        o, t,
                        sb["lnst"][:, 8 + c:9 + c],
                        sb["lnst"][:, 12 + c:13 + c],
                        ALU.mult, ALU.add),
                    waits=[st["v_ln"], tp_done[0], tp_done[1], xt_i[mt]]
                    + list(extra), inc="O"))
            return f

        def finish():
            lnst_done(st)
            return st["aff"]
        pieces = [p_sum(0, 1), p_sum(2, 3), p_sq(0), p_sq(1), p_sq(2),
                  p_sq(3), p_small(), p_aff(0), p_aff(1), p_aff(2), p_aff(3)]
        return pieces, st

    def lnst_done(st):
        nonlocal lnst_free, junk_sq_free
        lnst_free = st["v_ln"]
        junk_sq_free = [st["vsq"], st["vsq"]]

    def run_ln(seq_i, sc_i, extra=()):
        pieces, st = emit_ln_chain(seq_i, sc_i, extra)
        for p in pieces:
            p()
        lnst_done(st)
        return st["aff"]

    tp_done = [None, None]

    def emit_xn_tp(aff_toks):
        nonlocal rot
        xn_g0 = xn_ready = None
        last_tp = None
        for g in range(2):
            for mt in range(4):
                tmsl = sb["vb"][:, 1024 * mt:1024 * (mt + 1)]
                bk = rot % 8
                rot += 1
                for j in range(4):
                    p = 4 * g + j
                    last_tp = tp([aff_toks[mt], ctok["ident"],
                                  bankfree[bk], xn_free] if j == 0 else [],
                                 pb[bk][:, 128 * j:128 * (j + 1)],
                                 tmsl[:, 128 * p:128 * (p + 1)],
                                 inc="P" if j == 3 else None)
                xo = sb["xn"][:].rearrange("p (k t) -> p k t", k=8)[
                    :, 4 * g:4 * g + 4, 128 * mt:128 * (mt + 1)]
                xi_ap = pb[bk][:].rearrange("p (j t) -> p j t", j=4)
                xn_ready = E.op("scalar", lambda e, o=xo, i=xi_ap:
                                e.activation(o, i, AF.Copy),
                                waits=[last_tp], inc="A")
                bankfree[bk] = xn_ready
            if g == 0:
                xn_g0 = xn_ready
        tp_done[0] = last_tp
        tp_done[1] = last_tp
        return xn_g0, xn_ready

    def emit_gn(gn_last_tok):
        nonlocal seq_ab_tok
        denom = 1.0 / float(n_sc * SCW * D)
        gst = mm([gn_last_tok, ctok["sel"], bankfree[3]],
                 pb[3][0:2, 0:64], sb["sel"][:], sb["gn_acc"][:],
                 True, True, inc="P")
        g0 = E.op("vector", lambda e: e.tensor_copy(
            sb["gn_rows"][:], pb[3][0:2, 0:64]), waits=[gst], inc="V")
        bankfree[3] = g0
        g1 = E.op("vector", lambda e: e.tensor_add(
            sb["gn_der"][:, 0:16], sb["gn_rows"][:, 0:16],
            sb["gn_rows"][:, 16:32]), waits=[g0], inc="V")
        g2 = E.op("vector", lambda e: e.tensor_add(
            sb["gn_der"][:, 0:16], sb["gn_der"][:, 0:16],
            sb["gn_rows"][:, 32:48]), waits=[g1], inc="V")
        g3 = E.op("vector", lambda e: e.tensor_add(
            sb["gn_der"][:, 0:16], sb["gn_der"][:, 0:16],
            sb["gn_rows"][:, 48:64]), waits=[g2], inc="V")
        gv = sb["gn_der"][:].rearrange("p (a b) -> p a b", b=2)
        g4 = E.op("vector", lambda e: e.tensor_scalar(
            sb["gn_der"][:, 16:24].rearrange("p (a b) -> p a b", b=1),
            gv[:, 0:8, 0:1], denom, 0.0, ALU.mult, ALU.add),
            waits=[g3], inc="V")
        g5 = E.op("vector", lambda e: e.tensor_scalar(
            sb["gn_der"][:, 24:32].rearrange("p (a b) -> p a b", b=1),
            gv[:, 0:8, 1:2], denom, 0.0, ALU.mult, ALU.add),
            waits=[g4], inc="V")
        g6 = E.op("vector", lambda e: e.tensor_mul(
            sb["gn_der"][:, 32:40], sb["gn_der"][:, 16:24],
            sb["gn_der"][:, 16:24]), waits=[g5], inc="V")
        vv = E.op("vector", lambda e: e.tensor_sub(
            sb["gn_der"][:, 24:32], sb["gn_der"][:, 24:32],
            sb["gn_der"][:, 32:40]), waits=[g6], inc="V")
        aa = E.op("scalar", lambda e: e.activation(
            sb["gn_der"][:, 32:40], sb["gn_der"][:, 24:32], AF.Sqrt,
            bias=sb["epsb"][0:2, 0:1]), waits=[vv], inc="A")
        g7 = E.op("vector", lambda e: e.reciprocal(
            sb["gn_der"][:, 32:40], sb["gn_der"][:, 32:40]), waits=[aa],
            inc="V")
        gder = E.op("vector", lambda e: e.tensor_mul(
            sb["gn_der"][:, 40:48], sb["gn_der"][:, 16:24],
            sb["gn_der"][:, 32:40]), waits=[g7], inc="V")
        mm([gder, ctok["selT"], bankfree[4]],
           pb[4][:, 0:8], sb["selT"][:], sb["gn_der"][:, 32:40], True, True)
        gbc = mm([], pb[4][:, 8:16], sb["selT"][:], sb["gn_der"][:, 40:48],
                 True, True, inc="P")
        b0 = E.op("vector", lambda e: e.tensor_mul(
            sb["ab"][:, 0:8], pb[4][:, 0:8], sb["gnw"][:]),
            waits=[gbc, ctok["gnw"], ctok["gnb"]], inc="V")
        b1 = E.op("vector", lambda e: e.scalar_tensor_tensor(
            sb["ab"][:, 8:16], pb[4][:, 8:16], -1.0, sb["gnw"][:],
            ALU.mult, ALU.mult), waits=[b0], inc="V")
        seq_ab_tok = E.op("vector", lambda e: e.tensor_add(
            sb["ab"][:, 8:16], sb["ab"][:, 8:16], sb["gnb"][:]),
            waits=[b1], inc="V")
        bankfree[4] = seq_ab_tok


    def emit_pb_prep(seq_i, scb, xnbuf, ybbuf, xnfree_tok, ybfree):
        parb = (seq_i * n_sc + scb) % 2
        accb = xb[parb]
        ysrc2 = dram["s_y"][seq_i, :, SCW * scb:SCW * (scb + 1)].rearrange(
            "(pc p) t -> p pc t", p=128)
        ydst2 = ybbuf[:].rearrange("p (pc t) -> p pc t", pc=8)
        ylt = E.dma("scalar", "X", lambda e, o=ydst2, i=ysrc2:
                    e.dma_start(out=o, in_=i),
                    waits=[yscr_tok[(seq_i, scb)]] + ybfree)
        asrc2 = dram["s_acc"][seq_i, SCW * scb:SCW * (scb + 1), :].rearrange(
            "(mt p) c -> p mt c", p=128)
        adst2 = accb[:].rearrange("p (mt c) -> p mt c", mt=4)
        alt = E.dma("scalar", "X", lambda e, o=adst2, i=asrc2:
                    e.dma_start(out=o, in_=i),
                    waits=[accscr_tok[(seq_i, scb)], xbuf_free[parb]])
        yp_tok = None
        for pc in range(8):
            yp_tok = E.op("gpsimd", lambda e, pc=pc, xnb=xnbuf, ybb=ybbuf:
                          e.tensor_scalar(
                              xnb[:, 512 * pc:512 * (pc + 1)],
                              ybb[:, 512 * pc:512 * (pc + 1)],
                              sb["ab"][:, pc:pc + 1],
                              sb["ab"][:, 8 + pc:9 + pc],
                              ALU.mult, ALU.add),
                          waits=[ylt, seq_ab_tok, xnfree_tok] if pc == 0 else [],
                          inc="O" if pc == 7 else None)
        return (accb, parb, alt, yp_tok)

    load_x(0, 0)

    def direct_wmat(wnm, buf):
        toks = []
        for kc in range(8):
            toks.append(E.dma("gpsimd", "G", lambda e,
                              o=buf[:, 1024 * kc:1024 * (kc + 1)],
                              i=dram[wnm][128 * kc:128 * (kc + 1), :]:
                              e.dma_start(out=o, in_=i)))
        return toks
    wset_where["s_wq"] = (0, direct_wmat("wqT", wset[0]))
    pend_xn[(0, 0)] = run_ln(0, 0)
    wset_where["s_wk"] = (1, direct_wmat("wkT", wset[1]))
    wv0_tok = direct_wmat("wvT", sb["wdr"])

    def stage_rows(nm, snm, ntile, c0, c1, row0=0):
        toks = []
        for j in range(row0, row0 + ntile):
            toks.append(E.dma("gpsimd", "G", lambda e,
                              s=dram[nm][128 * j:128 * (j + 1), c0:c1],
                              d=dram[snm][128 * j:128 * (j + 1), c0:c1]:
                              e.dma_start(out=d, in_=s)))
        return toks
    stage_done[("s_wg", 0)] = stage_rows("wgT", "s_wg", 8, 0, 1024)
    stage_done[("s_wu", 0)] = stage_rows("wuT", "s_wu", 8, 0, 1024)
    stage_done[("s_wg", 1)] = stage_rows("wgT", "s_wg", 8, 1024, FF)
    stage_done[("s_wu", 1)] = stage_rows("wuT", "s_wu", 8, 1024, FF)
    for snm in ("s_wg", "s_wu"):
        stage_done[snm] = [stage_done[(snm, 0)], stage_done[(snm, 1)]]
    stage_done[("s_wd", 0)] = stage_rows("wdT", "s_wd", 8, 0, C, row0=0)
    stage_done["s_wq"] = stage_rows("wqT", "s_wq", 8, 0, C)
    stage_done["s_wk"] = stage_rows("wkT", "s_wk", 8, 0, C)
    stage_done[("s_wd", 1)] = stage_rows("wdT", "s_wd", 8, 0, C, row0=8)
    stage_done[("s_wd", 2)] = stage_rows("wdT", "s_wd", 8, 0, C, row0=16)
    stage_done["s_wv"] = stage_rows("wvT", "s_wv", 8, 0, C)
    stage_done["s_wo"] = stage_rows("woT", "s_wo", 8, 0, C)
    ha_free = None

    for seq in range(n_seq):
        gn_zero = E.op("vector", lambda e: e.memset(sb["gn_acc"][:], 0.0),
                       waits=[bankfree[3]], inc="V")

        for sc in range(n_sc):
            par = (seq * n_sc + sc) % 2
            xcur = xb[par]
            xt = x_tok[(seq, sc)]
            if sc + 1 < n_sc:
                load_x(seq, sc + 1)

            px = pend_xn.pop((seq, sc), None)
            if px is None:
                xn_g0, xn_ready = emit_xn_tp(run_ln(seq, sc))
            elif isinstance(px, tuple) and px and px[0] == "ready":
                xn_g0 = xn_ready = px[1]
            else:
                xn_g0, xn_ready = emit_xn_tp(px)

            load_win(0, win_free)
            load_win(1, win_free)

            mark(f"proj s{seq}c{sc}")
            slot_q, qtok = wset_where["s_wq"]
            slot_k, ktok = wset_where["s_wk"]
            qtev, ktev, vev = [], [], []

            q_done = None
            for m in range(8):
                bk = rot % 8
                rot += 1
                w = [xn_g0, qtok, bankfree[bk]]
                last = None
                for kc in range(8):
                    last = mm(w if kc == 0 else
                              ([xn_ready] if kc == 4 else []), pb[bk][:],
                              wset[slot_q][:, 1024 * kc + 128 * m:
                                           1024 * kc + 128 * (m + 1)],
                              sb["xn"][:, 512 * kc:512 * (kc + 1)],
                              kc == 0, kc == 7, inc="P" if kc == 7 else None)
                q_done = last
                tq = E.op("vector", lambda e, b=pb[bk], m=m: e.tensor_mul(
                    sb["qt"][:, 512 * m:512 * (m + 1)], b[:],
                    sb["qdec"][:, 512 * m:512 * (m + 1)]),
                    waits=[last, ctok["qdec"]], inc="V")
                bankfree[bk] = tq
                qtev.append(tq)
            defer_wset = []
            if seq == 0 and sc == 0:
                defer_wset.append((slot_q, "s_wq", q_done))
            else:
                load_wset(slot_q, "s_wv", q_done)

            k_done = None
            for m in range(8):
                bk = rot % 8
                rot += 1
                w = [xn_ready, ktok, bankfree[bk]]
                last = None
                for kc in range(8):
                    last = mm(w if kc == 0 else [], pb[bk][:],
                              wset[slot_k][:, 1024 * kc + 128 * m:
                                           1024 * kc + 128 * (m + 1)],
                              sb["xn"][:, 512 * kc:512 * (kc + 1)],
                              kc == 0, kc == 7, inc="P" if kc == 7 else None)
                k_done = last
                tk = E.op("vector", lambda e, b=pb[bk], m=m: e.tensor_mul(
                    sb["kt"][:, 512 * m:512 * (m + 1)], b[:],
                    sb["kdec"][:, 512 * m:512 * (m + 1)]),
                    waits=[last, ctok["kdec"]], inc="V")
                bankfree[bk] = tk
                ktev.append(tk)
            if seq == 0 and sc == 0:
                defer_wset.append((slot_k, "s_wk", k_done))
            elif sc + 1 < n_sc:
                load_wset(slot_k, "s_wq", k_done)
            else:
                load_wset(slot_k, "s_wo", k_done)

            if seq == 0 and sc == 0:
                slot_v, vtok, vsb = None, wv0_tok, sb["wdr"]
            else:
                slot_v, vtok = wset_where["s_wv"]
                vsb = wset[slot_v]
            v_done = None
            for mt in range(4):
                for ch in range(2):
                    bk = rot % 8
                    rot += 1
                    w = [vtok, bankfree[bk]]
                    last = None
                    for kc in range(8):
                        last = mm(w if kc == 0 else [], pb[bk][:],
                                  sb["xn"][:, 512 * kc + 128 * mt:
                                           512 * kc + 128 * (mt + 1)],
                                  vsb[:, 1024 * kc + 512 * ch:
                                      1024 * kc + 512 * (ch + 1)],
                                  kc == 0, kc == 7,
                                  inc="P" if kc == 7 else None)
                    v_done = last
                    tv = E.op("scalar", lambda e, b=pb[bk], mt=mt, ch=ch:
                              e.activation(
                                  sb["vb"][:, 1024 * mt + 512 * ch:
                                           1024 * mt + 512 * (ch + 1)],
                                  b[:], AF.Copy), waits=[last], inc="A")
                    bankfree[bk] = tv
                    vev.append(tv)
            if seq == 0 and sc == 0:
                for s4 in range(4):
                    wd_free[s4] = v_done
            elif sc + 1 < n_sc:
                load_wset(slot_v, "s_wk", v_done)
            elif seq + 1 < n_seq:
                load_wset(slot_v, "s_wq", v_done)

            mark(f"ktm s{seq}c{sc}")
            ktm_last = None
            for mt in range(4):
                for g in range(2):
                    bk = rot % 8
                    rot += 1
                    last_tp = None
                    for j in range(4):
                        p = 4 * g + j
                        last_tp = tp([ktev[-1], bankfree[bk]] if j == 0
                                     else [],
                                     pb[bk][:, 128 * j:128 * (j + 1)],
                                     sb["kt"][:, 512 * p + 128 * mt:
                                              512 * p + 128 * (mt + 1)],
                                     inc="P" if j == 3 else None)
                    ktm_last = E.op("scalar", lambda e,
                                    o=sb["ha"][:, 1024 * mt + 512 * g:
                                               1024 * mt + 512 * (g + 1)],
                                    i=pb[bk][:]: e.activation(o, i, AF.Copy),
                                    waits=[last_tp, ha_free], inc="A")
                    bankfree[bk] = ktm_last

            mark(f"ret s{seq}c{sc}")
            sd_off = [0, 512, 896, 1152]
            sd_base = [4096, 4096 + 1280]
            sd_free = [None, None]
            y_free = [bankfree[7]]
            ret_rot = [0]
            yev_last = None
            s_all = {}
            sd_all = {}
            ydone = {}

            def emit_sbatch(pair, hh):
                part = 64 * hh
                for kc in range(NKC):
                    N = SCW - LK * kc
                    bk = ret_rot[0] % 7
                    ret_rot[0] += 1
                    s_all[(pair, hh, kc)] = (bk, N, mm(
                        [ktev[pair], qtev[pair], bankfree[bk]],
                        pb[bk][:, 0:N],
                        sb["kt"][part:part + 64,
                                 512 * pair + 128 * kc:
                                 512 * pair + 128 * (kc + 1)],
                        sb["qt"][part:part + 64,
                                 512 * pair + 128 * kc:
                                 512 * (pair + 1)],
                        True, True, inc="P"))

            def emit_sdevac(pair, hh):
                SPLIT = globals().get("SD_SPLIT", False)
                for kc in range(NKC):
                    bk, N, s_tok = s_all[(pair, hh, kc)]
                    base = sd_base[hh] + sd_off[kc]
                    if not SPLIT:
                        dg = E.op("vector", lambda e, b=pb[bk], N=N,
                                  o=sb["ha"][:, base:base + N]:
                                  e.tensor_mul(o, b[:, 0:N],
                                               sb["mask"][:, 0:N]),
                                  waits=[s_tok, sd_free[hh], ctok["mask"],
                                         ha_free], inc="V")
                        cp = None
                    else:
                        dg = E.op("vector", lambda e, b=pb[bk],
                                  o=sb["ha"][:, base:base + 128]:
                                  e.tensor_mul(o, b[:, 0:128],
                                               sb["mask"][:, 0:128]),
                                  waits=[s_tok, sd_free[hh], ctok["mask"],
                                         ha_free], inc="V")
                        if N > 128:
                            cp = E.op("scalar", lambda e, b=pb[bk], N=N,
                                      o=sb["ha"][:, base + 128:base + N]:
                                      e.activation(o, b[:, 128:N], AF.Copy),
                                      waits=[s_tok, sd_free[hh]], inc="A")
                        else:
                            cp = None
                    bankfree[bk] = [dg] if cp is None else [dg, cp]
                    sd_all[(pair, hh, kc)] = (dg, cp)

            def emit_ychain(pair, hh):
                h = 2 * pair + hh
                part = 64 * hh
                started = False
                if sc > 0:
                    mm([sbf_tok, qtev[pair], y_free[0]],
                       pb[7][part:part + 64, 0:512],
                       sb["Sbf"][part:part + 64, 64 * pair:64 * pair + 64],
                       sb["qt"][part:part + 64, 512 * pair:512 * (pair + 1)],
                       True, False)
                    started = True
                y_tok = None
                for kc in range(NKC):
                    N = SCW - LK * kc
                    dg, cp = sd_all[(pair, hh, kc)]
                    wy = [dg, cp, vev[-1]]
                    if not started:
                        wy.append(y_free[0])
                    y_tok = mm(wy, pb[7][part:part + 64, 128 * kc:512],
                               sb["vb"][:, 1024 * kc + 64 * h:
                                        1024 * kc + 64 * (h + 1)],
                               sb["ha"][:, sd_base[hh] + sd_off[kc]:
                                        sd_base[hh] + sd_off[kc] + N],
                               not started, kc == NKC - 1,
                               inc="P" if kc == NKC - 1 else None)
                    started = True
                sd_free[hh] = y_tok
                ydone[(pair, hh)] = y_tok

            PIPE = globals().get("RET_PIPE", False)
            if PIPE:
                emit_sbatch(0, 0)
                emit_sdevac(0, 0)
                emit_sbatch(0, 1)
            for pair in range(8):
                if PIPE:
                    emit_sdevac(pair, 1)
                    if pair + 1 < 8:
                        emit_sbatch(pair + 1, 0)
                    emit_ychain(pair, 0)
                    if pair + 1 < 8:
                        emit_sdevac(pair + 1, 0)
                        emit_sbatch(pair + 1, 1)
                    emit_ychain(pair, 1)
                else:
                    emit_sbatch(pair, 0)
                    emit_sdevac(pair, 0)
                    emit_sbatch(pair, 1)
                    emit_ychain(pair, 0)
                    emit_sdevac(pair, 1)
                    emit_ychain(pair, 1)
                y_tok = ydone[(pair, 1)]
                col = 16 * sc + 2 * pair
                jh = pair % 2
                a_cp = E.op("scalar", lambda e, p=pair: e.activation(
                    sb["yb"][:, 512 * p:512 * (p + 1)], pb[7][:], AF.Copy),
                    waits=[y_tok, ysb_free, gn_prev], inc="A")
                yev_last = E.op("scalar", lambda e, j=jh: e.activation(
                    sb["junk"][:, 512 * j:512 * (j + 1)], pb[7][:], AF.Square),
                    waits=[junk_sq_free[jh]], inc="A")
                y_free[0] = yev_last
                E.op("vector", lambda e, p=pair, c=col: e.reduce_sum(
                    sb["gn_acc"][:, c:c + 1], sb["yb"][:, 512 * p:512 * (p + 1)],
                    axis=mybir.AxisListType.X), waits=[a_cp, gn_zero])
                gnred = E.op("vector", lambda e, c=col, j=jh: e.reduce_sum(
                    sb["gn_acc"][:, c + 1:c + 2],
                    sb["junk"][:, 512 * j:512 * (j + 1)],
                    axis=mybir.AxisListType.X), waits=[yev_last], inc="V")
                junk_sq_free[jh] = gnred
            bankfree[7] = y_free[0]
            gn_last = gnred
            gn_prev = gn_last

            mark(f"state s{seq}c{sc}")
            st_tok = None
            w0 = [ktm_last, vev[-1], bankfree[6]]
            for pair in range(8):
                for hh in range(2):
                    h = 2 * pair + hh
                    for kc in range(NKC):
                        st_tok = mm(w0 if (pair == 0 and hh == 0 and kc == 0)
                                    else [],
                                    pb[6][64 * hh:64 * hh + 64,
                                          64 * pair:64 * pair + 64],
                                    sb["ha"][:, 1024 * kc + 64 * h:
                                             1024 * kc + 64 * (h + 1)],
                                    sb["vb"][:, 1024 * kc + 64 * h:
                                             1024 * kc + 64 * (h + 1)],
                                    kc == 0, kc == NKC - 1,
                                    inc="P" if (pair == 7 and hh == 1 and
                                                kc == NKC - 1) else None)
            if sc == 0:
                s1 = E.op("vector", lambda e: e.tensor_mul(
                    sb["Shat"][:], pb[6][:], sb["sdec"][:]),
                    waits=[st_tok, ctok["sdec"]], inc="V")
            else:
                s0 = E.op("vector", lambda e: e.tensor_add(
                    sb["Shat"][:], pb[6][:], sb["Shat"][:]),
                    waits=[st_tok, sbf_tok], inc="V")
                s1 = E.op("vector", lambda e: e.tensor_mul(
                    sb["Shat"][:], sb["Shat"][:], sb["sdec"][:]),
                    waits=[s0], inc="V")
            sbf_tok = E.op("vector", lambda e: e.tensor_copy(
                sb["Sbf"][:], sb["Shat"][:]), waits=[s1], inc="V")
            bankfree[6] = sbf_tok

            ysrc = sb["yb"][:].rearrange("p (pc t) -> p pc t", pc=8)
            ydst = dram["s_y"][seq, :, SCW * sc:SCW * (sc + 1)].rearrange(
                "(pc p) t -> p pc t", p=128)
            yscr_tok[(seq, sc)] = E.dma("gpsimd", "G",
                                        lambda e, o=ydst, i=ysrc:
                                        e.dma_start(out=o, in_=i),
                                        waits=[yev_last])
            ysb_free = yscr_tok[(seq, sc)]
            if sc == n_sc - 1:
                emit_gn(gn_last)
                st3 = st_tok

            mark(f"ffn s{seq}c{sc}")
            rot5 = rot
            wd_quads = {}

            def wd_slot_ap(s):
                if s < 4:
                    return sb["wdr"][:, 2048 * s:2048 * (s + 1)]
                if s < 6:
                    return sb["wgw"][:, 2048 * (s - 4):2048 * (s - 3)]
                return sb["wuw"][:, 2048 * (s - 6):2048 * (s - 5)]

            def load_quad(j):
                par_q = 0 if (seq == 0 and sc == 0) else wd_pair[0] % 2
                slot = (wd_pair[0] % 8)
                wd_pair[0] += 1
                jj = j % 12
                src_ap = dram["s_wd"][256 * jj:256 * (jj + 1), :].rearrange(
                    "(k p) c -> p k c", p=128)
                dst_ap = wd_slot_ap(slot).rearrange("p (k c) -> p k c", k=2)
                wt = E.dma("sync" if par_q == 0 else "gpsimd",
                           "W" if par_q == 0 else "G",
                           lambda e, o=dst_ap, i=src_ap:
                           e.dma_start(out=o, in_=i),
                           waits=[stage_done[("s_wd", jj // 4)],
                                  wd_free[slot]])
                wd_quads[j] = (slot, wt)
            wd_pair[0] += (-wd_pair[0]) % 8
            for jq in range(4):
                load_quad(jq)
            h_toks = [None] * NFT
            gu_last = None
            for w in range(NWIN):
                for m2 in range(2):
                    fm = 2 * w + m2
                    wslot = w % 2
                    bkg = rot5 % 8
                    rot5 += 1
                    wg = [win_tok[w][0], xn_ready, bankfree[bkg]]
                    g_last = None
                    for kc in range(8):
                        g_last = mm(wg if kc == 0 else [], pb[bkg][:],
                                    sb["wgw"][:, 2048 * wslot + 256 * kc +
                                              128 * m2:
                                              2048 * wslot + 256 * kc +
                                              128 * (m2 + 1)],
                                    sb["xn"][:, 512 * kc:512 * (kc + 1)],
                                    kc == 0, kc == 7,
                                    inc="P" if kc == 7 else None)
                    bku = rot5 % 8
                    rot5 += 1
                    wu = [win_tok[w][1], bankfree[bku]]
                    u_last = None
                    for kc in range(8):
                        u_last = mm(wu if kc == 0 else [], pb[bku][:],
                                    sb["wuw"][:, 2048 * wslot + 256 * kc +
                                              128 * m2:
                                              2048 * wslot + 256 * kc +
                                              128 * (m2 + 1)],
                                    sb["xn"][:, 512 * kc:512 * (kc + 1)],
                                    kc == 0, kc == 7,
                                    inc="P" if kc == 7 else None)
                    gu_last = u_last
                    sg = E.op("scalar", lambda e, b=pb[bkg], fm=fm:
                              e.activation(
                                  sb["silu2"][:, 512 * (fm % 2):
                                              512 * (fm % 2 + 1)],
                                  b[:], AF.Silu, scale=INV127),
                              waits=[g_last,
                                     h_toks[fm - 2] if fm >= 2 else None],
                              inc="A")
                    bankfree[bkg] = sg
                    wh = [u_last, sg, ha_free]
                    if fm < 8:
                        wh.append(st_tok)
                    elif fm < 13:
                        wh.append(sd_free[0])
                        wh.append(sd_free[1])
                    h_toks[fm] = E.op("vector", lambda e, b=pb[bku], fm=fm:
                                      e.scalar_tensor_tensor(
                                          sb["ha"][:, 512 * fm:512 * (fm + 1)],
                                          b[:], INV127,
                                          sb["silu2"][:, 512 * (fm % 2):
                                                      512 * (fm % 2 + 1)],
                                          ALU.mult, ALU.mult),
                                      waits=wh, inc="V")
                    bankfree[bku] = h_toks[fm]
                if w + 2 < NWIN:
                    load_win(w + 2, gu_last)
                if sc + 1 < n_sc:
                    if w == 0:
                        ln_pieces = emit_ln_chain(seq, sc + 1,
                                                  extra=[st_tok])
                    if w >= 1 and ln_pieces[0]:
                        ln_pieces[0].pop(0)()
                        if not ln_pieces[0]:
                            lnst_done(ln_pieces[1])
                            pend_xn[(seq, sc + 1)] = ln_pieces[1]["aff"]
            rot = rot5
            win_free = gu_last
            xn_free = gu_last
            if sc + 1 < n_sc:
                if ln_pieces[0]:
                    for p in ln_pieces[0]:
                        p()
                    lnst_done(ln_pieces[1])
                    pend_xn[(seq, sc + 1)] = ln_pieces[1]["aff"]
            else:
                pb_prep0 = emit_pb_prep(seq, 0, sb["xn"], sb["yb"],
                                        gu_last, [ysb_free, gn_last])

            for dslot, dsnm, dtok in defer_wset:
                load_wset(dslot, dsnm, dtok)

            mark(f"down s{seq}c{sc}")
            for s in range(4, 8):
                if wd_free[s] is None or True:
                    wd_free[s] = [wd_free[s], gu_last]
            fo_last = None
            down_tok = None
            regs8 = [(0, 0), (0, 1), (1, 0), (1, 1), (2, 0), (2, 1),
                     (3, 0), (3, 1)]
            for kc in range(24):
                qi = kc // 2
                q3, wt = wd_quads[qi]
                for ri, (mt, ch) in enumerate(regs8):
                    bk = ri
                    w = [wt, h_toks[kc]]
                    if kc == 0:
                        w.append(bankfree[bk])
                    down_tok = mm(w, pb[bk][:],
                                  sb["ha"][:, 512 * kc + 128 * mt:
                                           512 * kc + 128 * (mt + 1)],
                                  wd_slot_ap(q3)[:,
                                            1024 * (kc % 2) + 512 * ch:
                                            1024 * (kc % 2) + 512 * ch +
                                            512],
                                  kc == 0, kc == 23,
                                  inc="P" if ri == len(regs8) - 1
                                  else None)
                if kc % 2 == 1:
                    wd_free[q3] = down_tok
                    if qi + 4 < 12:
                        load_quad(qi + 4)
            for ri, (mt, ch) in enumerate(regs8):
                fo_last = E.op("vector", lambda e, b=pb[ri], xc=xcur,
                               mt=mt, ch=ch:
                               e.scalar_tensor_tensor(
                                   xc[:, 1024 * mt + 512 * ch:
                                      1024 * mt + 512 * (ch + 1)],
                                   b[:], INV127,
                                   xc[:, 1024 * mt + 512 * ch:
                                      1024 * mt + 512 * (ch + 1)],
                                   ALU.mult, ALU.add),
                               waits=[down_tok, xt], inc="V")
                bankfree[ri] = fo_last
            if "d_ha" in dram and seq == 0:
                down_tok = E.dma("gpsimd", "G",
                                 lambda e, o=dram["d_ha"][sc], i=sb["ha"][:]:
                                 e.dma_start(out=o, in_=i), waits=[down_tok])
            ha_free = down_tok
            win_free = down_tok

            asrc = xcur[:].rearrange("p (mt c) -> p mt c", mt=4)
            adst = dram["s_acc"][seq, SCW * sc:SCW * (sc + 1), :].rearrange(
                "(mt p) c -> p mt c", p=128)
            accscr_tok[(seq, sc)] = E.dma("gpsimd", "G",
                                          lambda e, o=adst, i=asrc:
                                          e.dma_start(out=o, in_=i),
                                          waits=[fo_last])
            xbuf_free[par] = accscr_tok[(seq, sc)]

        mark(f"passB s{seq}")
        lnp_next = None
        slot_wo, wo_tok = wset_where["s_wo"]
        prep = pb_prep0
        ao_hist = [st3, st3]
        yp_hist = [None, None]
        for scb in range(n_sc):
            accb, parb, alt, yp_tok = prep
            xnbuf = sb["xn"] if scb % 2 == 0 else sb["qt"]
            yp_hist[scb % 2] = yp_tok
            out_ev = None
            oev_mt = [None] * 4
            ao_last = None
            for mt in range(4):
                for ch in range(2):
                    bk = rot % 8
                    rot += 1
                    w = [yp_tok, wo_tok, bankfree[bk]]
                    last = None
                    for kc in range(8):
                        last = mm(w if kc == 0 else [], pb[bk][:],
                                  xnbuf[:, 512 * kc + 128 * mt:
                                        512 * kc + 128 * (mt + 1)],
                                  wset[slot_wo][:, 1024 * kc + 512 * ch:
                                               1024 * kc + 512 * (ch + 1)],
                                  kc == 0, kc == 7,
                                  inc="P" if kc == 7 else None)
                    ao_last = last
                    out_ev = E.op("vector", lambda e, b=pb[bk], ac=accb,
                                  mt=mt, ch=ch:
                                  e.tensor_add(
                                      ac[:, 1024 * mt + 512 * ch:
                                         1024 * mt + 512 * (ch + 1)],
                                      b[:],
                                      ac[:, 1024 * mt + 512 * ch:
                                         1024 * mt + 512 * (ch + 1)]),
                                  waits=[last, alt], inc="V")
                    oev_mt[mt] = out_ev
                    bankfree[bk] = out_ev
                    if lnp_next is not None and lnp_next[0]:
                        lnp_next[0].pop(0)()
                        if lnp_next[0]:
                            lnp_next[0].pop(0)()
            xn_free = ao_last
            if scb + 1 < n_sc:
                nb = scb + 1
                prep = emit_pb_prep(
                    seq, nb,
                    sb["xn"] if nb % 2 == 0 else sb["qt"],
                    sb["yb"] if nb % 2 == 0 else sb["kt"],
                    ao_hist[nb % 2],
                    [yp_hist[nb % 2], st3, yscr_tok[(seq, n_sc - 1)]])
            ao_hist[scb % 2] = ao_last
            ot = []
            for mt in range(4):
                ot.append(E.dma("gpsimd", "G", lambda e,
                                o=dram["out"][seq, SCW * scb + 128 * mt:
                                              SCW * scb + 128 * (mt + 1), :],
                                i=accb[:, 1024 * mt:1024 * (mt + 1)]:
                                e.dma_start(out=o, in_=i),
                                waits=[oev_mt[mt]]))
            out_toks.append(ot)
            xbuf_free[parb] = ot
            if scb == n_sc - 1 and lnp_next is not None:
                for p in lnp_next[0]:
                    p()
                lnst_done(lnp_next[1])
                pend_xn[(seq + 1, 0)] = lnp_next[1]["aff"]
            if scb == 2 and seq + 1 < n_seq:
                load_x(seq + 1, 0)
                lnp_next = emit_ln_chain(seq + 1, 0, extra=[st3])
        ysb_free = yp_hist[0]
        if seq + 1 < n_seq:
            load_wset(slot_wo, "s_wk", xn_free)

    E.wait("gpsimd", out_toks)


def kernel(L0_attn_x, L0_attn_wq, L0_attn_wk, L0_attn_wv, L0_attn_wo,
           L0_attn_gn_w, L0_attn_gn_b, L0_ffn_w_gate, L0_ffn_w_up,
           L0_ffn_w_down):
    x = np.ascontiguousarray(np.asarray(L0_attn_x))
    consts = make_consts()
    wqT = np.ascontiguousarray(np.asarray(L0_attn_wq).T)
    wkT = np.ascontiguousarray(np.asarray(L0_attn_wk).T)
    wvT = np.ascontiguousarray(np.asarray(L0_attn_wv).T)
    woT = np.ascontiguousarray(np.asarray(L0_attn_wo).T)
    wgT = np.ascontiguousarray(np.asarray(L0_ffn_w_gate).T)
    wuT = np.ascontiguousarray(np.asarray(L0_ffn_w_up).T)
    wdT = np.ascontiguousarray(np.asarray(L0_ffn_w_down).T)
    gnw = np.ascontiguousarray(np.asarray(L0_attn_gn_w).reshape(8, 128).T)
    gnb = np.ascontiguousarray(np.asarray(L0_attn_gn_b).reshape(8, 128).T)
    nc = build_nc()
    in_maps = []
    for core in range(NCORES):
        m = dict(x=np.ascontiguousarray(x[BSEQ * core:BSEQ * (core + 1)]),
                 wqT=wqT, wkT=wkT, wvT=wvT, woT=woT, wgT=wgT, wuT=wuT, wdT=wdT,
                 gnw=gnw, gnb=gnb)
        m.update(consts)
        in_maps.append(m)
    res = run_bass_kernel_spmd(nc, in_maps, core_ids=list(range(NCORES)))
    return np.concatenate([r["out"] for r in res.results], axis=0)
